# version 06
# baseline (speedup 1.0000x reference)
import numpy as np
import concourse.bass as bass
import concourse.mybir as mybir
from concourse.bass_utils import run_bass_kernel_spmd

F32 = mybir.dt.float32
BF16 = mybir.dt.bfloat16
I32 = mybir.dt.int32
AF = mybir.ActivationFunctionType
ALU = mybir.AluOpType
AX = mybir.AxisListType.X

NEG = -1.0e30
T, D, H, HD, W, M, I = 4096, 2048, 16, 128, 256, 4096, 8192
TT = T // 128
DC = D // 128
MC = M // 128
IC = I // 128
NCORE = 8

_DBG_OUTS = set()
_PHASES_UPTO = None
_PHASE_SELECT = None
_P0_PARTS = {"masks", "gate", "casts", "scales", "conv_gu", "conv_d"}
_LAST_RESULTS = None

ENGS = ("t", "v", "s", "g", "y")


class _Sem:
    __slots__ = ("h", "n")

    def __init__(self, nc, name):
        self.h = nc.alloc_semaphore(name)
        self.n = 0


class _PL:

    def __init__(self, nc):
        self.nc = nc
        self.q = {e: [] for e in ENGS}
        self.c = {e: _Sem(nc, "c" + e) for e in ENGS}
        self.dirty = {e: set() for e in ENGS}
        self.guards = {}
        self._sems = {}
        self.marks = [{e: 0 for e in ENGS}]

    def sem(self, name):
        if name not in self._sems:
            self._sems[name] = _Sem(self.nc, name)
        return self._sems[name]

    def wait(self, e, sem, val=None):
        v = sem.n if val is None else val
        if v <= 0:
            return
        self.q[e].append(lambda b, s=sem.h, v=v: b.wait_ge(s, v))

    def inc(self, e, sem, n=1):
        sem.n += n
        self.q[e].append(lambda b, s=sem.h, n=n: b.sem_inc(s, n))

    def dma(self, e, out, in_, sem, transpose=False):
        sem.n += 16
        self.dirty[e].add(sem)

        def f(b, o=out, i=in_, s=sem.h, tr=transpose):
            if tr:
                b.dma_start(out=o, in_=i, transpose=True).then_inc(s, 16)
            else:
                b.dma_start(out=o, in_=i).then_inc(s, 16)

        self.q[e].append(f)

    def op(self, e, meth, *a, inc=None, **kw):
        auto = e in ("v", "s", "g")
        if auto:
            self.wait(e, self.c[e], self.c[e].n)
            assert inc is None or inc is self.c[e]
            inc = self.c[e]
        if inc is not None:
            inc.n += 1

        def f(b, meth=meth, a=a, kw=kw, inc=inc):
            r = getattr(b, meth)(*a, **kw)
            if inc is not None:
                r.then_inc(inc.h, 1)

        self.q[e].append(f)

    def gset(self, key, eng):
        self.guards[key] = (eng, self.c[eng].n)

    def gwait(self, e, key):
        if key in self.guards:
            eng, val = self.guards[key]
            if eng == e:
                return
            self.wait(e, self.c[eng], val)

    def boundary(self):
        for e in ENGS:
            for s in self.dirty[e]:
                self.wait(e, s)
            self.dirty[e] = set()
        self.guards = {}
        self.marks.append({e: len(self.q[e]) for e in ENGS})


def _build():
    nc = bass.Bass("TRN2", target_bir_lowering=False)

    def din(name, shape, dt):
        return nc.dram_tensor(name, shape, dt, kind="ExternalInput").ap()

    def dscr(name, shape, dt):
        kind = "ExternalOutput" if name in _DBG_OUTS else "Internal"
        return nc.dram_tensor(name, shape, dt, kind=kind).ap()

    x_h = din("x", [T, D], F32)
    wq_h = din("wq", [D, D], F32)
    wk_h = din("wk", [D, D], F32)
    wv_h = din("wv", [D, D], F32)
    wo_h = din("wo", [D, D], F32)
    gw_h = din("gw", [1, D], F32)
    gb_h = din("gb", [1], F32)
    mem_h = din("mem", [M, D], F32)
    fg_h = din("fg", [I, D], I32)
    fgs_h = din("fgs", [I], F32)
    fu_h = din("fu", [I, D], I32)
    fus_h = din("fus", [I], F32)
    fd_h = din("fd", [D, I], I32)
    fds_h = din("fds", [D], F32)
    y_h = nc.dram_tensor("y", [T, D], F32, kind="ExternalOutput").ap()

    wqN = dscr("wqN", [D, D], BF16)
    wkN = dscr("wkN", [D, D], BF16)
    wvN = dscr("wvN", [D, D], BF16)
    woN = dscr("woN", [D, D], BF16)
    memN = dscr("memN", [M, D], BF16)
    wgN = dscr("wgN", [I, D], BF16)
    wuN = dscr("wuN", [I, D], BF16)
    wdN = dscr("wdN", [D, I], BF16)
    vmT = dscr("vmT", [D, M], BF16)
    vo = dscr("vo", [M, D], BF16)
    xlnT = dscr("xlnT", [D, T], BF16)
    qTs = dscr("qTs", [D, T], BF16)
    kTs = dscr("kTs", [D, T], BF16)
    qmTs = dscr("qmTs", [D, T], BF16)
    vts = dscr("vts", [T, D], BF16)
    oloc = dscr("oloc", [T, D], F32)
    sBs = dscr("sBs", [T, M], BF16)
    out1 = dscr("out1", [T, D], F32)
    gateRs = dscr("gateRs", [128, D], F32)
    yTs = dscr("yTs", [D, T], BF16)
    hTs = dscr("hTs", [I, T], BF16)
    pAs = dscr("pAs", [T, D], F32)

    _uid = [0]

    def sb_at(off, shape, dt):
        _uid[0] += 1
        assert off % 32 == 0
        return nc.alloc_sbuf_tensor_at(
            "sb%d" % _uid[0], shape, dt, offset=off).ap()

    base0 = (nc.sbuf_base + 31) // 32 * 32
    cap = nc.sbuf_top

    class Ar:
        def __init__(self, start):
            self.off = start

        def t(self, shape, dt):
            nbytes = 1
            for s in shape[1:]:
                nbytes *= s
            nbytes *= {F32: 4, BF16: 2, I32: 4}[dt]
            nbytes = (nbytes + 31) // 32 * 32
            r = sb_at(self.off, shape, dt)
            self.off += nbytes
            assert self.off <= cap, f"sbuf overflow {self.off} > {cap}"
            return r

        def tl(self, n, shape, dt):
            return [self.t(shape, dt) for _ in range(n)]

    per = Ar(base0)
    ident = per.t([128, 128], BF16)
    gbT = per.t([128, 1], F32)
    gscT = per.t([128, IC], F32)
    uscT = per.t([128, IC], F32)
    dscT = per.t([128, DC], F32)
    bypAll = per.t([128, TT], F32)
    l8All = per.t([128, TT], F32)
    c_eps = per.t([128, 1], F32)
    c_invD = per.t([128, 1], F32)
    c_sHD = per.t([128, 1], F32)
    c_sD = per.t([128, 1], F32)
    pbase = (per.off + 31) // 32 * 32

    ps7 = nc.alloc_psum_tensor("ps7", [128, 3072], F32).ap()
    psTf = nc.alloc_psum_tensor("psT", [128, 2048], BF16).ap()

    def bank(k, w=512, o=0):
        return ps7[:, 512 * k + o:512 * k + o + w]

    def psT_half(j):
        return psTf[:, 1024 * j:1024 * j + 1024]

    pl = _PL(nc)
    S = pl.sem

    def conv_gu(wsrc, wdst, ar, tag):
        bi = ar.tl(2, [128, D], I32)
        bo = ar.tl(2, [128, D], BF16)
        sbi = [S(tag + "i0"), S(tag + "i1")]
        sbo = [S(tag + "o0"), S(tag + "o1")]
        state = [0]

        def step():
            ic = state[0]
            if ic >= IC:
                return
            state[0] += 1
            sl = ic % 2
            pl.wait("y", sbo[sl])
            pl.dma("y", bi[sl][:, :], wsrc[ic * 128:(ic + 1) * 128, :], sbi[sl])
            pl.wait("v", sbi[sl])
            pl.op("v", "tensor_copy", out=bo[sl], in_=bi[sl])
            pl.wait("s", pl.c["v"], pl.c["v"].n)
            pl.dma("s", wdst[ic * 128:(ic + 1) * 128, :], bo[sl][:, :], sbo[sl])
        return step

    def conv_d(ar):
        di = ar.tl(2, [128, 4096], I32)
        do = ar.tl(2, [128, 4096], BF16)
        sdi = [S("cdi0"), S("cdi1")]
        sdo = [S("cdo0"), S("cdo1")]
        state = [0]

        def step():
            k = state[0]
            if k >= DC * 2:
                return
            state[0] += 1
            hc, half = k // 2, k % 2
            sl = k % 2
            pl.wait("y", sdo[sl])
            pl.dma("y", di[sl][:, :],
                   fd_h[hc * 128:(hc + 1) * 128,
                        half * 4096:(half + 1) * 4096], sdi[sl])
            pl.wait("v", sdi[sl])
            pl.op("v", "tensor_scalar", out=do[sl], in0=di[sl],
                  scalar1=-128.0, scalar2=dscT[:, hc:hc + 1],
                  op0=ALU.add, op1=ALU.mult)
            pl.wait("s", pl.c["v"], pl.c["v"].n)
            pl.dma("s", wdN[hc * 128:(hc + 1) * 128,
                         half * 4096:(half + 1) * 4096], do[sl][:, :],
                   sdo[sl])
        return step

    def p01():
        a = Ar(pbase)
        pl.op("v", "memset", c_eps, 1e-5)
        pl.op("v", "memset", c_invD, 1.0 / D)
        pl.op("v", "memset", c_sHD, 1.0 / float(np.sqrt(HD)))
        pl.op("v", "memset", c_sD, 1.0 / float(np.sqrt(D)))
        ones1 = a.t([1, 128], F32)
        grow = a.t([1, D], F32)
        gbrow = a.t([1, 1], F32)
        gateR0 = a.t([128, D], F32)
        sq = S("p0q")
        sg = S("p0g")
        pl.op("g", "memset", ident, 1.0, inc=pl.c["g"])
        pl.wait("g", pl.c["g"], pl.c["g"].n)
        pl.op("g", "affine_select", out=ident, in_=ident, pattern=[[1, 128]],
              compare_op=ALU.is_equal, fill=0.0, base=0, channel_multiplier=-1,
              inc=pl.c["g"])
        pl.dma("g", wqN[:, :], wq_h[:, :], sg)
        pl.dma("g", wkN[:, :], wk_h[:, :], sg)
        pl.dma("g", wvN[:, :], wv_h[:, :], sg)
        pl.dma("g", woN[:, :], wo_h[:, :], sg)
        pl.dma("g", memN[:, :], mem_h[:, :], sg)
        pl.dma("y", grow[:, :], gw_h[:, :], sq)
        pl.dma("y", gbrow[:, :], gb_h.rearrange("(a b) -> a b", b=1), sq)
        for c in range(IC):
            pl.dma("y", gscT[:, c:c + 1],
                   fgs_h[c * 128:(c + 1) * 128].rearrange("(p o) -> p o", o=1), sq)
            pl.dma("y", uscT[:, c:c + 1],
                   fus_h[c * 128:(c + 1) * 128].rearrange("(p o) -> p o", o=1), sq)
        for c in range(DC):
            pl.dma("y", dscT[:, c:c + 1],
                   fds_h[c * 128:(c + 1) * 128].rearrange("(p o) -> p o", o=1), sq)
        pl.op("v", "memset", ones1, 1.0, inc=pl.c["v"])
        pl.wait("t", pl.c["v"], pl.c["v"].n)
        pl.wait("t", sq)
        pl.wait("t", pl.c["g"], pl.c["g"].n)
        pl.op("t", "matmul", bank(0, 1), ones1[0:1, :], gbrow[0:1, :],
              start=True, stop=True, inc=pl.c["t"])
        pl.wait("v", pl.c["t"], pl.c["t"].n)
        pl.op("v", "tensor_copy", out=gbT, in_=bank(0, 1))
        pl.gset("bank0", "v")
        for jsb in range(4):
            pl.gwait("t", "bank1")
            pl.op("t", "matmul", bank(1), ones1[0:1, :],
                  grow[0:1, jsb * 512:(jsb + 1) * 512], start=True,
                  stop=True, inc=pl.c["t"])
            pl.wait("v", pl.c["t"], pl.c["t"].n)
            pl.op("v", "tensor_copy",
                  out=gateR0[:, jsb * 512:(jsb + 1) * 512],
                  in_=bank(1))
            pl.gset("bank1", "v")
        pl.wait("s", pl.c["v"], pl.c["v"].n)
        pl.dma("s", gateRs[:, :], gateR0[:, :], S("p0gr"))
        xt = a.tl(2, [128, D], F32)
        sqd = a.t([128, D], F32)
        xln = a.t([128, D], BF16)
        slab = a.t([128, DC * 512], BF16)
        small = a.tl(5, [128, 1], F32)
        sx = [S("lnx0"), S("lnx1")]
        sst = S("lnst")
        slab_v = slab.rearrange("p (dc tl c) -> p dc tl c", dc=DC, tl=4)
        for tt in range(TT):
            sl = tt % 2
            pl.gwait("y", "xt%d" % sl)
            pl.dma("y", xt[sl][:, :], x_h[tt * 128:(tt + 1) * 128, :], sx[sl])
            pl.wait("v", sx[sl])
            pl.guards.pop("cent", None)
            ln_tile((small[0], small[1], small[2], small[3], small[4], sqd),
                    xt[sl], xt[sl], xln)
            pl.gset("xt%d" % sl, "v")
            tl = tt % 4
            for hb in range(2):
                psT = psT_half(hb)
                psT_v = psT.rearrange("p (k o c) -> p k o c", k=8, o=1)
                pl.gwait("t", "psT%d" % hb)
                pl.wait("t", pl.c["v"], pl.c["v"].n)
                if tl == 0 and hb == 0:
                    pl.wait("t", sst, sst.n)
                for k in range(8):
                    pl.op("t", "transpose", psT[:, k * 128:(k + 1) * 128],
                          xln[:, (hb * 8 + k) * 128:(hb * 8 + k + 1) * 128],
                          ident, inc=(pl.c["t"] if k == 7 else None))
                pl.wait("v", pl.c["t"], pl.c["t"].n)
                pl.op("v", "tensor_copy",
                      out=slab_v[:, hb * 8:hb * 8 + 8, tl:tl + 1, :], in_=psT_v)
                pl.gset("psT%d" % hb, "v")
            if tl == 3:
                pl.wait("s", pl.c["v"], pl.c["v"].n)
                g = tt // 4
                for dc in range(DC):
                    pl.dma("s", xlnT[dc * 128:(dc + 1) * 128,
                                   g * 512:(g + 1) * 512],
                           slab[:, dc * 512:(dc + 1) * 512], sst)

    def p0d1():
        a = Ar(pbase)
        wvT = a.tl(DC, [128, D], BF16)
        memTh = a.tl(DC, [128, 2048], BF16)
        stg = a.tl(2, [128, 512], BF16)
        sw = S("d1w")
        sm = S("d1m")
        sst = [S("d1s0"), S("d1s1")]
        for ec in range(DC):
            pl.dma("s", wvT[ec][:, :], wvN[:, ec * 128:(ec + 1) * 128], sw,
                   transpose=True)
        for mh in range(2):
            if mh == 1:
                pl.gwait("s", "memTh")
            for ec in range(DC):
                pl.dma("s", memTh[ec][:, :],
                       memN[mh * 2048:(mh + 1) * 2048, ec * 128:(ec + 1) * 128],
                       sm, transpose=True)
            pl.wait("t", sw)
            pl.wait("t", sm)
            for jc in range(DC):
                for ms in range(4):
                    bk = ms
                    pl.gwait("t", "bank%d" % bk)
                    for ec in range(DC):
                        last = ec == DC - 1
                        pl.op("t", "matmul", bank(bk),
                              wvT[ec][:, jc * 128:(jc + 1) * 128],
                              memTh[ec][:, ms * 512:(ms + 1) * 512],
                              start=(ec == 0), stop=last,
                              inc=(pl.c["t"] if last else None))
                    sl = ms % 2
                    pl.wait("v", pl.c["t"], pl.c["t"].n)
                    pl.wait("v", sst[sl])
                    pl.op("v", "tensor_copy", out=stg[sl], in_=bank(bk),
                          inc=pl.c["v"])
                    pl.gset("bank%d" % bk, "v")
                    pl.wait("s", pl.c["v"], pl.c["v"].n)
                    pl.dma("s", vmT[jc * 128:(jc + 1) * 128,
                                  mh * 2048 + ms * 512:mh * 2048 + (ms + 1) * 512],
                           stg[sl][:, :], sst[sl])
            pl.gset("memTh", "t")

    def p0d2():
        a = Ar(pbase)
        woT = a.tl(DC, [128, D], BF16)
        slab = a.tl(DC, [128, 512], BF16)
        stg = a.tl(2, [128, D], BF16)
        sw = S("d2w")
        sm = S("d2m")
        sst = [S("d2s0"), S("d2s1")]
        for vc in range(DC):
            pl.dma("s", woT[vc][:, :], woN[:, vc * 128:(vc + 1) * 128], sw,
                   transpose=True)
        pl.wait("t", sw)
        for mg in range(8):
            pl.gwait("y", "d2slab")
            for vc in range(DC):
                pl.dma("y", slab[vc][:, :],
                       vmT[vc * 128:(vc + 1) * 128, mg * 512:(mg + 1) * 512], sm)
            pl.wait("t", sm)
            for ml in range(4):
                sl = ml % 2
                pl.gwait("t", "d2stg%d" % sl)
                pl.wait("t", sst[sl], sst[sl].n)
                for bk in range(4):
                    pl.gwait("t", "bank%d" % bk)
                for vc in range(DC):
                    for os in range(4):
                        last = vc == DC - 1
                        pl.op("t", "matmul", bank(os),
                              slab[vc][:, ml * 128:(ml + 1) * 128],
                              woT[vc][:, os * 512:(os + 1) * 512],
                              start=(vc == 0), stop=last,
                              inc=(pl.c["t"] if (last and os == 3) else None))
                pl.wait("v", pl.c["t"], pl.c["t"].n)
                for os in range(4):
                    pl.op("v", "tensor_copy", out=stg[sl][:, os * 512:(os + 1) * 512],
                          in_=bank(os), inc=(pl.c["v"] if os == 3 else None))
                for bk in range(4):
                    pl.gset("bank%d" % bk, "v")
                pl.gset("d2stg%d" % sl, "v")
                pl.wait("s", pl.c["v"], pl.c["v"].n)
                pl.dma("s", vo[(mg * 4 + ml) * 128:(mg * 4 + ml + 1) * 128, :],
                       stg[sl][:, :], sst[sl])
            pl.gset("d2slab", "t")

    def ln_tile(a_bufs, src_sbuf, cent, dst_bf16):
        sums, mean, vs, sdv, rstd, sqd = a_bufs
        pl.op("v", "reduce_sum", out=sums, in_=src_sbuf, axis=AX)
        pl.op("v", "tensor_scalar", out=mean, in0=sums, scalar1=1.0 / D,
              scalar2=None, op0=ALU.mult)
        pl.gwait("v", "cent")
        pl.op("v", "tensor_scalar", out=cent, in0=src_sbuf, scalar1=mean,
              scalar2=None, op0=ALU.subtract)
        pl.wait("s", pl.c["v"], pl.c["v"].n)
        pl.op("s", "activation", out=sqd, in_=cent, func=AF.Square,
              accum_out=vs)
        pl.op("s", "activation", out=sdv, in_=vs, func=AF.Sqrt,
              scale=c_invD[:, 0:1], bias=c_eps[:, 0:1])
        pl.wait("v", pl.c["s"], pl.c["s"].n)
        pl.op("v", "reciprocal", out=rstd, in_=sdv)
        pl.gwait("v", "lnout")
        pl.op("v", "tensor_scalar", out=dst_bf16, in0=cent, scalar1=rstd,
              scalar2=None, op0=ALU.mult)

    def proj_phase(wsrc, transposed, dst, scale, tag, conv=None):
        scale_ap = 1.0
        if scale != 1.0:
            scale_ap = {1.0 / float(np.sqrt(HD)): c_sHD,
                        1.0 / float(np.sqrt(D)): c_sD}[scale][:, 0:1]
        a = Ar(pbase)
        wt = a.tl(DC, [128, D], BF16)
        xs = [a.tl(DC, [128, 512], BF16) for _ in range(2)]
        stg = a.tl(4, [128, 512], BF16)
        cstep = None
        if conv == "g":
            cstep = conv_gu(fg_h, wgN, a, "cvg")
        elif conv == "u":
            cstep = conv_gu(fu_h, wuN, a, "cvu")
        elif conv == "d":
            cstep = conv_d(a)
        sw = S(tag + "w")
        sxs = [S(tag + "x0"), S(tag + "x1")]
        sst = [S(tag + "s%d" % i) for i in range(4)]
        for dc in range(DC):
            if transposed:
                pl.dma("s", wt[dc][:, :], wsrc[:, dc * 128:(dc + 1) * 128], sw,
                       transpose=True)
            else:
                pl.dma("s", wt[dc][:, :], wsrc[dc * 128:(dc + 1) * 128, :], sw)
        pl.wait("t", sw)
        for ts in range(8):
            sl = ts % 2
            pl.gwait("y", tag + "xs%d" % sl)
            for dc in range(DC):
                pl.dma("y", xs[sl][dc][:, :],
                       xlnT[dc * 128:(dc + 1) * 128, ts * 512:(ts + 1) * 512],
                       sxs[sl])
            pl.wait("t", sxs[sl])
            for jc in range(DC):
                bk = jc % 4
                ee = "v" if jc % 2 == 0 else "s"
                pl.gwait("t", "bank%d" % bk)
                pl.wait("t", sst[bk], sst[bk].n)
                for dc in range(DC):
                    last = dc == DC - 1
                    pl.op("t", "matmul", bank(bk),
                          wt[dc][:, jc * 128:(jc + 1) * 128], xs[sl][dc][:, :],
                          start=(dc == 0), stop=last,
                          inc=(pl.c["t"] if last else None))
                pl.wait(ee, pl.c["t"], pl.c["t"].n)
                if ee == "v":
                    pl.op("v", "tensor_scalar", out=stg[bk], in0=bank(bk),
                          scalar1=scale, scalar2=None, op0=ALU.mult,
                          inc=pl.c["v"])
                else:
                    pl.op("s", "activation", out=stg[bk], in_=bank(bk),
                          func=AF.Copy, scale=scale_ap, inc=pl.c["s"])
                pl.gset("bank%d" % bk, ee)
                pl.wait("s", pl.c[ee], pl.c[ee].n)
                pl.dma("s", dst[jc * 128:(jc + 1) * 128, ts * 512:(ts + 1) * 512],
                       stg[bk][:, :], sst[bk])
                if cstep is not None and (jc % 2 == 1):
                    cstep()
            pl.gset(tag + "xs%d" % sl, "t")
        if cstep is not None:
            for _ in range(2 * IC):
                cstep()

    def p2d():
        a = Ar(pbase)
        wvT = a.tl(DC, [128, D], BF16)
        xs = [a.tl(DC, [128, 512], BF16) for _ in range(2)]
        stg = a.tl(2, [128, D], BF16)
        sw = S("vtw")
        sxs = [S("vtx0"), S("vtx1")]
        sst = [S("vts0"), S("vts1")]
        for dc in range(DC):
            pl.dma("s", wvT[dc][:, :], wvN[:, dc * 128:(dc + 1) * 128], sw,
                   transpose=True)
        pl.wait("t", sw)
        for g in range(8):
            sl = g % 2
            pl.gwait("y", "vtxs%d" % sl)
            for dc in range(DC):
                pl.dma("y", xs[sl][dc][:, :],
                       xlnT[dc * 128:(dc + 1) * 128, g * 512:(g + 1) * 512],
                       sxs[sl])
            pl.wait("t", sxs[sl])
            for tl in range(4):
                tt = g * 4 + tl
                ssl = tt % 2
                pl.wait("t", sst[ssl], sst[ssl].n)
                pl.gwait("t", "vtstg%d" % ssl)
                for bk in range(4):
                    pl.gwait("t", "bank%d" % bk)
                for dc in range(DC):
                    for js in range(4):
                        last = dc == DC - 1
                        pl.op("t", "matmul", bank(js),
                              xs[sl][dc][:, tl * 128:(tl + 1) * 128],
                              wvT[dc][:, js * 512:(js + 1) * 512],
                              start=(dc == 0), stop=last,
                              inc=(pl.c["t"] if (last and js == 3) else None))
                pl.wait("v", pl.c["t"], pl.c["t"].n)
                for js in range(4):
                    pl.op("v", "tensor_copy",
                          out=stg[ssl][:, js * 512:(js + 1) * 512], in_=bank(js),
                          inc=(pl.c["v"] if js == 3 else None))
                for bk in range(4):
                    pl.gset("bank%d" % bk, "v")
                pl.gset("vtstg%d" % ssl, "v")
                pl.wait("s", pl.c["v"], pl.c["v"].n)
                pl.dma("s", vts[tt * 128:(tt + 1) * 128, :], stg[ssl][:, :],
                       sst[ssl])
            pl.gset("vtxs%d" % sl, "t")

    def p3():
        a = Ar(pbase)
        woT = a.tl(DC, [128, D], BF16)
        gateR = a.t([128, D], F32)
        maskG = a.t([128, 384], F32)
        mask1 = a.t([128, 256], F32)
        mask0 = a.t([128, 128], F32)
        m4G = a.t([128, 4 * 384], F32)
        iv = a.t([128, 384], F32)
        tmp = a.t([128, 384], F32)
        qsl = a.tl(H, [128, 512], BF16)
        ksl = a.tl(H, [128, 768], BF16)
        vsl = a.tl(6, [128, D], BF16)
        _ssb_off = a.off
        ssb4d = a.tl(2, [128, 4 * 384], F32)
        pE4d = a.tl(2, [128, 4 * 384], F32)
        Pb4d = a.tl(2, [128, 4 * 384], BF16)
        PT = a.t([128, 4 * 384], BF16)
        ctxq = a.tl(4, [128, 512], BF16)
        ostg = a.tl(2, [128, D], F32)
        negm4d = a.tl(2, [128, 4], F32)
        lsum4d = a.tl(2, [128, 4], F32)
        rl4d = a.tl(2, [128, 4], F32)
        bl = a.t([128, 1], F32)
        junk = sb_at(_ssb_off, [128, D], F32)
        sw = S("p3w")
        sq = S("p3q")
        sk = S("p3k")
        sv = S("p3v")
        sst = [S("p3s0"), S("p3s1")]
        pl.op("g", "iota", iv, [[1, 384]], base=0, channel_multiplier=-1,
              allow_small_or_imprecise_dtypes=True, inc=pl.c["g"])
        pl.wait("v", pl.c["g"], pl.c["g"].n)
        pl.op("v", "tensor_scalar", out=tmp, in0=iv, scalar1=1.0,
              scalar2=None, op0=ALU.is_ge)
        pl.op("v", "tensor_scalar", out=maskG, in0=iv, scalar1=256.0,
              scalar2=None, op0=ALU.is_le)
        pl.op("v", "tensor_tensor", out=maskG, in0=maskG, in1=tmp, op=ALU.mult)
        pl.op("v", "tensor_scalar", out=maskG, in0=maskG, scalar1=-1.0,
              scalar2=-NEG, op0=ALU.add, op1=ALU.mult)
        pl.op("v", "tensor_scalar", out=mask1, in0=iv[:, 0:256], scalar1=128.0,
              scalar2=None, op0=ALU.is_le)
        pl.op("v", "tensor_scalar", out=mask1, in0=mask1, scalar1=-1.0,
              scalar2=-NEG, op0=ALU.add, op1=ALU.mult)
        pl.op("v", "tensor_scalar", out=mask0, in0=iv[:, 0:128], scalar1=0.0,
              scalar2=None, op0=ALU.is_le)
        pl.op("v", "tensor_scalar", out=mask0, in0=mask0, scalar1=-1.0,
              scalar2=-NEG, op0=ALU.add, op1=ALU.mult)
        for j in range(4):
            pl.op("v", "tensor_copy", out=m4G[:, j * 384:(j + 1) * 384],
                  in_=maskG)
        for dc in range(DC):
            pl.dma("s", woT[dc][:, :], woN[:, dc * 128:(dc + 1) * 128], sw,
                   transpose=True)
        pl.dma("y", gateR[:, :], gateRs[:, :], S("p3gr"))
        pl.wait("v", S("p3gr"))
        pl.wait("t", sw)

        for g in range(8):
            for h in range(H):
                pl.gwait("y", "p3qk")
                pl.dma("y", qsl[h][:, :],
                       qTs[h * 128:(h + 1) * 128, g * 512:(g + 1) * 512], sq)
                lo = g * 512 - 256
                off = 0
                if lo < 0:
                    off = -lo
                    lo = 0
                pl.dma("y", ksl[h][:, off:768],
                       kTs[h * 128:(h + 1) * 128, lo:g * 512 + 512], sk)
            for tl4 in range(4):
                tt = g * 4 + tl4
                pl.gwait("y", "p3qk")
                pl.dma("y", vsl[tt % 6][:, :],
                       vts[tt * 128:(tt + 1) * 128, :], sv)
            pl.wait("t", sq)
            pl.wait("t", sk)
            pl.wait("t", sv)
            for tl4 in range(4):
                tt = g * 4 + tl4
                kcc = min(tt + 1, 3)
                fc = tt - kcc + 1
                m4 = m4G
                msk1h = maskG if tt >= 2 else (mask1 if tt == 1 else mask0)
                wmask = kcc * 128
                pb_done = {}

                def scores_quad(qd):
                    for j in range(4):
                        h = qd * 4 + j
                        pl.gwait("t", "sbank%d" % j)
                        for kc in range(kcc):
                            slabc = (fc + kc) - (g * 4 - 2)
                            pl.op("t", "matmul", bank(j, 128, o=kc * 128),
                                  qsl[h][:, tl4 * 128:(tl4 + 1) * 128],
                                  ksl[h][:, slabc * 128:(slabc + 1) * 128],
                                  start=True, stop=True,
                                  inc=(pl.c["t"] if (kc == kcc - 1 and j == 3)
                                       else None))
                    return pl.c["t"].n

                def backend_quad(qd):
                    qp = qd % 2
                    Pb4 = Pb4d[qp]
                    pl.wait("t", pl.c["v"], pb_done[qd])
                    for hb2 in range(2):
                        psT = psT_half(hb2)
                        pl.gwait("t", "psT%d" % hb2)
                        for jj in range(2):
                            j = hb2 * 2 + jj
                            for kc in range(kcc):
                                lastt = (jj == 1 and kc == kcc - 1)
                                pl.op("t", "transpose",
                                      psT[:, (jj * kcc + kc) * 128:
                                          (jj * kcc + kc + 1) * 128],
                                      Pb4[:, j * 384 + kc * 128:
                                          j * 384 + (kc + 1) * 128],
                                      ident,
                                      inc=(pl.c["t"] if lastt else None))
                        pl.wait("v", pl.c["t"], pl.c["t"].n)
                        pl.gwait("v", "PTh%d" % hb2)
                        pl.op("v", "tensor_copy",
                              out=PT[:, hb2 * 768:hb2 * 768 + 2 * wmask],
                              in_=psT[:, 0:2 * wmask])
                        pl.gset("psT%d" % hb2, "v")
                    pl.gset("Pb4_%d" % qp, "t")
                    pl.wait("t", pl.c["v"], pl.c["v"].n)
                    pl.gwait("t", "bank4")
                    for j in range(4):
                        h = qd * 4 + j
                        hb2, jj = j // 2, j % 2
                        for kc in range(kcc):
                            pl.op("t", "matmul", bank(4, 128, o=j * 128),
                                  vsl[(fc + kc) % 6][:, h * 128:(h + 1) * 128],
                                  PT[:, hb2 * 768 + (jj * kcc + kc) * 128:
                                      hb2 * 768 + (jj * kcc + kc + 1) * 128],
                                  start=(kc == 0), stop=(kc == kcc - 1),
                                  inc=(pl.c["t"] if (j == 3 and kc == kcc - 1)
                                       else None))
                    pl.gset("PTh0", "t")
                    pl.gset("PTh1", "t")
                    pl.wait("v", pl.c["t"], pl.c["t"].n)
                    pl.gwait("v", "ctxq%d" % qd)
                    pl.op("v", "tensor_copy", out=ctxq[qd], in_=bank(4))
                    pl.gset("bank4", "v")

                def front1(qd):
                    qp = qd % 2
                    ps_v = ps7[:, 0:2048].rearrange(
                        "p (h c) -> p h c", h=4)[:, :, 0:wmask]
                    ssb_v = ssb4d[qp].rearrange(
                        "p (h c) -> p h c", h=4)[:, :, 0:wmask]
                    m4_v = m4.rearrange(
                        "p (h c) -> p h c", h=4)[:, :, 0:wmask]
                    pl.wait("v", pl.c["t"], sd[qd])
                    pl.gwait("v", "ssb4_%d" % qp)
                    pl.gwait("v", "negm4_%d" % qp)
                    if tt >= 2:
                        pl.op("v", "tensor_tensor", out=ssb_v, in0=ps_v,
                              in1=m4_v, op=ALU.add)
                    else:
                        for j in range(4):
                            pl.op("v", "tensor_tensor",
                                  out=ssb4d[qp][:, j * 384:j * 384 + wmask],
                                  in0=bank(j, wmask),
                                  in1=msk1h[:, 0:wmask], op=ALU.add)
                    pl.op("v", "reduce_max", out=negm4d[qp], in_=ssb_v,
                          axis=AX, negate=True)
                    add_done[qd] = pl.c["v"].n
                    for j in range(4):
                        pl.gset("sbank%d" % j, "v")

                def front2(qd):
                    qp = qd % 2
                    pl.wait("s", pl.c["v"], add_done[qd])
                    pl.gwait("s", "lsum4_%d" % qp)
                    for j in range(4):
                        pl.op("s", "activation",
                              out=pE4d[qp][:, j * 384:j * 384 + wmask],
                              in_=ssb4d[qp][:, j * 384:j * 384 + wmask],
                              func=AF.Exp, bias=negm4d[qp][:, j:j + 1],
                              accum_out=lsum4d[qp][:, j:j + 1])
                    exp_done[qd] = pl.c["s"].n
                    pl.gset("ssb4_%d" % qp, "s")
                    pl.gset("negm4_%d" % qp, "s")

                def front3(qd):
                    qp = qd % 2
                    pl.wait("v", pl.c["s"], exp_done[qd])
                    pl.gwait("v", "rl4_%d" % qp)
                    pl.op("v", "reciprocal", out=rl4d[qp], in_=lsum4d[qp])
                    pl.gset("lsum4_%d" % qp, "v")
                    pl.gwait("v", "Pb4_%d" % qp)
                    for j in range(4):
                        pl.op("v", "tensor_scalar",
                              out=Pb4d[qp][:, j * 384:j * 384 + wmask],
                              in0=pE4d[qp][:, j * 384:j * 384 + wmask],
                              scalar1=rl4d[qp][:, j:j + 1], scalar2=None,
                              op0=ALU.mult)
                    pb_done[qd] = pl.c["v"].n
                    pl.gset("rl4_%d" % qp, "v")

                sd = {}
                add_done = {}
                exp_done = {}
                sd[0] = scores_quad(0)
                front1(0)
                sd[1] = scores_quad(1)
                front2(0)
                front3(0)
                for qd in range(1, 4):
                    front1(qd)
                    if qd < 3:
                        sd[qd + 1] = scores_quad(qd + 1)
                    front2(qd)
                    front3(qd)
                    backend_quad(qd - 1)
                backend_quad(3)
                sl = tt % 2
                pl.wait("t", pl.c["v"], pl.c["v"].n)
                pl.wait("t", pl.c["s"], pl.c["s"].n)
                pl.wait("t", sst[sl], sst[sl].n)
                pl.gwait("t", "ostg%d" % sl)
                for half2 in range(2):
                    pl.gwait("t", "bank4")
                    pl.gwait("t", "bank5")
                    for hc in range(H):
                        last = hc == H - 1
                        for jj in range(2):
                            js = half2 * 2 + jj
                            pl.op("t", "matmul", bank(4 + jj),
                                  ctxq[hc // 4][:, (hc % 4) * 128:
                                                (hc % 4 + 1) * 128],
                                  woT[hc][:, js * 512:(js + 1) * 512],
                                  start=(hc == 0), stop=last,
                                  inc=(pl.c["t"] if (last and jj == 1) else None))
                    pl.wait("v", pl.c["t"], pl.c["t"].n)
                    for jj in range(2):
                        js = half2 * 2 + jj
                        pl.op("v", "tensor_copy",
                              out=ostg[sl][:, js * 512:(js + 1) * 512],
                              in_=bank(4 + jj))
                    pl.gset("bank4", "v")
                    pl.gset("bank5", "v")
                for q2 in range(4):
                    pl.gset("ctxq%d" % q2, "t")
                pl.gwait("v", "bl")
                pl.op("v", "tensor_tensor", out=junk, in0=ostg[sl],
                      in1=gateR, op=ALU.mult)
                pl.op("v", "reduce_sum", out=bl, in_=junk, axis=AX)
                pl.wait("s", pl.c["v"], pl.c["v"].n)
                pl.op("s", "activation", out=bypAll[:, tt:tt + 1], in_=bl,
                      func=AF.Sigmoid, bias=gbT[:, 0:1])
                pl.gset("bl", "s")
                pl.dma("s", oloc[tt * 128:(tt + 1) * 128, :], ostg[sl][:, :],
                       sst[sl])
                pl.gset("ostg%d" % sl, "v")
            pl.gset("p3qk", "t")

    def p4():
        a = Ar(pbase)
        memT = a.tl(DC, [128, M], BF16)
        qsl0 = a.tl(DC, [128, 512], BF16)
        qsl = [qsl0, qsl0]
        simSd = a.tl(2, [128, M], F32)
        sE = a.t([128, M], BF16)
        sB = a.t([128, M], BF16)
        t8 = a.t([128, 8], F32)
        negm = a.t([128, 1], F32)
        e8 = a.t([128, 8], F32)
        sm = S("p4m")
        sq = [S("p4q0"), S("p4q1")]
        sst = S("p4st")
        for ec in range(DC):
            pl.dma("s", memT[ec][:, :], memN[:, ec * 128:(ec + 1) * 128], sm,
                   transpose=True)
        pl.wait("t", sm)
        for g in range(8):
            sl = g % 2
            pl.gwait("y", "p4xs")
            for ec in range(DC):
                pl.dma("y", qsl[sl][ec][:, :],
                       qmTs[ec * 128:(ec + 1) * 128, g * 512:(g + 1) * 512],
                       sq[sl])
            pl.wait("t", sq[sl])
            for tl4 in range(4):
                tt = g * 4 + tl4
                sp = tt % 2
                simS = simSd[sp]
                for mq in range(8):
                    bk = mq % 4
                    ee = "v" if mq % 2 == 0 else "s"
                    pl.gwait("t", "bank%d" % bk)
                    if mq == 0:
                        pl.gwait("t", "simS%d" % sp)
                    for ec in range(DC):
                        last = ec == DC - 1
                        pl.op("t", "matmul", bank(bk),
                              qsl[sl][ec][:, tl4 * 128:(tl4 + 1) * 128],
                              memT[ec][:, mq * 512:(mq + 1) * 512],
                              start=(ec == 0), stop=last,
                              inc=(pl.c["t"] if last else None))
                    pl.wait(ee, pl.c["t"], pl.c["t"].n)
                    pl.gwait(ee, "simSr%d" % sp)
                    if ee == "v":
                        pl.op("v", "tensor_copy",
                              out=simS[:, mq * 512:(mq + 1) * 512],
                              in_=bank(bk))
                    else:
                        pl.op("s", "activation",
                              out=simS[:, mq * 512:(mq + 1) * 512],
                              in_=bank(bk), func=AF.Copy)
                    pl.gset("bank%d" % bk, ee)
                pl.wait("v", pl.c["s"], pl.c["s"].n)
                pl.gwait("v", "t8")
                pl.op("v", "max", out=t8, in_=simS)
                pl.op("v", "reduce_max", out=negm, in_=t8, axis=AX,
                      negate=True)
                pl.wait("s", pl.c["v"], pl.c["v"].n)
                pl.gwait("s", "sE")
                pl.op("s", "activation", out=e8, in_=t8, func=AF.Exp,
                      bias=negm[:, 0:1], accum_out=l8All[:, tt:tt + 1])
                pl.op("s", "activation", out=sE, in_=simS, func=AF.Exp,
                      bias=negm[:, 0:1])
                pl.wait("v", pl.c["s"], pl.c["s"].n)
                pl.wait("v", sst, sst.n)
                pl.op("v", "scalar_tensor_tensor", out=sB, in0=simS,
                      scalar=t8[:, 7:8], in1=sE, op0=ALU.is_ge, op1=ALU.mult)
                pl.gset("simS%d" % sp, "v")
                pl.gset("simSr%d" % sp, "v")
                pl.gset("sE", "v")
                pl.gset("t8", "v")
                pl.wait("s", pl.c["v"], pl.c["v"].n)
                pl.dma("s", sBs[tt * 128:(tt + 1) * 128, :], sB[:, :], sst)
            pl.gset("p4xs", "t")

    def p5():
        a = Ar(pbase)
        voT = a.tl(MC, [128, D], BF16)
        sBt = a.tl(2, [128, M], BF16)
        sTball = a.t([128, MC * 128], BF16)
        xt = a.t([128, D], F32)
        olt = a.t([128, D], F32)
        o1 = a.t([128, D], F32)
        cent = a.t([128, D], F32)
        y2 = a.t([128, D], BF16)
        yslab = a.t([128, DC * 256], BF16)
        scl = a.t([128, 1], F32)
        small = a.tl(5, [128, 1], F32)
        svo = S("p5vo")
        ssb = [S("p5b0"), S("p5b1")]
        sx = S("p5x")
        sol = S("p5ol")
        so1 = S("p5o1")
        syl = S("p5yl")
        yslab_v = yslab.rearrange("p (dc tl c) -> p dc tl c", dc=DC, tl=2)
        for mc in range(MC):
            pl.dma("y", voT[mc][:, :], vo[mc * 128:(mc + 1) * 128, :], svo)
        pl.wait("t", svo)
        for tt in range(TT):
            sl = tt % 2
            pl.gwait("y", "sBt%d" % sl)
            pl.dma("y", sBt[sl][:, :], sBs[tt * 128:(tt + 1) * 128, :], ssb[sl])
            pl.gwait("y", "xt")
            pl.dma("y", xt[:, :], x_h[tt * 128:(tt + 1) * 128, :], sx)
            pl.gwait("y", "olt")
            pl.dma("y", olt[:, :], oloc[tt * 128:(tt + 1) * 128, :], sol)
            pl.wait("t", ssb[sl])
            pl.gwait("t", "sTb")
            for mc in range(MC):
                k = mc % 8
                hbs = (mc // 8) % 2
                psT = psT_half(hbs)
                if k == 0:
                    pl.gwait("t", "psT%d" % hbs)
                pl.op("t", "transpose", psT[:, k * 128:(k + 1) * 128],
                      sBt[sl][:, mc * 128:(mc + 1) * 128], ident,
                      inc=(pl.c["t"] if k == 7 else None))
                if k == 7:
                    pl.wait("v", pl.c["t"], pl.c["t"].n)
                    pl.op("v", "tensor_copy",
                          out=sTball[:, (mc - 7) * 128:(mc + 1) * 128],
                          in_=psT[:, 0:1024])
                    pl.gset("psT%d" % hbs, "v")
            pl.gset("sBt%d" % sl, "t")
            pl.wait("t", pl.c["v"], pl.c["v"].n)
            for bk in range(4):
                pl.gwait("t", "bank%d" % bk)
            for mc in range(MC):
                for js in range(4):
                    last = mc == MC - 1
                    pl.op("t", "matmul", bank(js),
                          sTball[:, mc * 128:(mc + 1) * 128],
                          voT[mc][:, js * 512:(js + 1) * 512],
                          start=(mc == 0), stop=last,
                          inc=(pl.c["t"] if (last and js == 3) else None))
            pl.gset("sTb", "t")
            pl.op("v", "reciprocal", out=scl, in_=l8All[:, tt:tt + 1])
            pl.op("v", "tensor_tensor", out=scl, in0=scl,
                  in1=bypAll[:, tt:tt + 1], op=ALU.mult)
            pl.wait("v", sx)
            pl.wait("v", sol)
            pl.wait("v", so1, so1.n)
            pl.gwait("v", "o1")
            pl.op("v", "tensor_tensor", out=o1, in0=xt, in1=olt, op=ALU.add,
                  inc=pl.c["v"])
            pl.gset("xt", "v")
            pl.wait("s", pl.c["v"], pl.c["v"].n)
            pl.wait("s", pl.c["t"], pl.c["t"].n)
            for js in range(4):
                pl.op("s", "activation", out=olt[:, js * 512:(js + 1) * 512],
                      in_=bank(js), func=AF.Copy, scale=scl[:, 0:1],
                      inc=(pl.c["s"] if js == 3 else None))
            for bk in range(4):
                pl.gset("bank%d" % bk, "s")
            pl.wait("v", pl.c["s"], pl.c["s"].n)
            pl.op("v", "tensor_tensor", out=o1, in0=o1, in1=olt, op=ALU.add,
                  inc=pl.c["v"])
            pl.gset("olt", "v")
            pl.gset("scl", "v")
            pl.wait("s", pl.c["v"], pl.c["v"].n)
            pl.dma("s", out1[tt * 128:(tt + 1) * 128, :], o1[:, :], so1)
            pl.guards.pop("cent", None)
            pl.guards.pop("lnout", None)
            ln_tile((small[0], small[1], small[2], small[3], small[4], olt),
                    o1, cent, y2)
            pl.gset("o1", "v")
            pl.gset("olt", "s")
            tl = tt % 2
            for hb in range(2):
                psT = psT_half(hb)
                psT_v = psT.rearrange("p (k o c) -> p k o c", k=8, o=1)
                pl.gwait("t", "psT%d" % hb)
                pl.wait("t", pl.c["v"], pl.c["v"].n)
                if tl == 0 and hb == 0:
                    pl.wait("t", syl, syl.n)
                for k in range(8):
                    dc = hb * 8 + k
                    pl.op("t", "transpose", psT[:, k * 128:(k + 1) * 128],
                          y2[:, dc * 128:(dc + 1) * 128], ident,
                          inc=(pl.c["t"] if k == 7 else None))
                pl.wait("v", pl.c["t"], pl.c["t"].n)
                pl.op("v", "tensor_copy",
                      out=yslab_v[:, hb * 8:hb * 8 + 8, tl:tl + 1, :], in_=psT_v)
                pl.gset("psT%d" % hb, "v")
            if tl == 1:
                pl.wait("s", pl.c["v"], pl.c["v"].n)
                g2 = tt // 2
                for dc in range(DC):
                    pl.dma("s", yTs[dc * 128:(dc + 1) * 128,
                                  g2 * 256:(g2 + 1) * 256],
                           yslab[:, dc * 256:(dc + 1) * 256], syl)

    def p6():
        a = Ar(pbase)
        yT = a.tl(DC, [128, T], BF16)
        wgi = a.tl(2, [128, D], BF16)
        wui = a.tl(2, [128, D], BF16)
        wgT = a.t([128, D], BF16)
        wuT = a.t([128, D], BF16)
        gS = a.t([128, 512], BF16)
        uS = a.t([128, 512], BF16)
        hS = a.tl(2, [128, 512], BF16)
        syt = S("p6y")
        sgi = [S("p6g0"), S("p6g1")]
        sui = [S("p6u0"), S("p6u1")]
        sho = [S("p6h0"), S("p6h1")]
        for dc in range(DC):
            pl.dma("y", yT[dc][:, :], yTs[dc * 128:(dc + 1) * 128, :], syt)
        pl.wait("t", syt)
        for ic in range(IC):
            sl = ic % 2
            pl.gwait("y", "p6w%d" % sl)
            pl.dma("y", wgi[sl][:, :], wgN[ic * 128:(ic + 1) * 128, :], sgi[sl])
            pl.dma("y", wui[sl][:, :], wuN[ic * 128:(ic + 1) * 128, :], sui[sl])
            pl.wait("t", sgi[sl])
            pl.wait("t", sui[sl])
            pl.gwait("t", "wT")
            for win, wtb in ((wgi[sl], wgT), (wui[sl], wuT)):
                for hb in range(2):
                    psT = psT_half(hb)
                    pl.gwait("t", "psT%d" % hb)
                    for k in range(8):
                        hc = hb * 8 + k
                        pl.op("t", "transpose", psT[:, k * 128:(k + 1) * 128],
                              win[:, hc * 128:(hc + 1) * 128], ident,
                              inc=(pl.c["t"] if k == 7 else None))
                    pl.wait("v", pl.c["t"], pl.c["t"].n)
                    pl.op("v", "tensor_scalar",
                          out=wtb[:, hb * 1024:(hb + 1) * 1024],
                          in0=psT[:, 0:1024], scalar1=-128.0,
                          scalar2=None, op0=ALU.add)
                    pl.gset("psT%d" % hb, "v")
            pl.gset("p6w%d" % sl, "t")
            pl.wait("t", pl.c["v"], pl.c["v"].n)
            for ts in range(8):
                bg = ts % 3
                bu = 3 + ts % 3
                hsl = ts % 2
                pl.gwait("t", "bank%d" % bg)
                pl.gwait("t", "bank%d" % bu)
                for hc in range(DC):
                    last = hc == DC - 1
                    pl.op("t", "matmul", bank(bg),
                          wgT[:, hc * 128:(hc + 1) * 128],
                          yT[hc][:, ts * 512:(ts + 1) * 512],
                          start=(hc == 0), stop=last)
                    pl.op("t", "matmul", bank(bu),
                          wuT[:, hc * 128:(hc + 1) * 128],
                          yT[hc][:, ts * 512:(ts + 1) * 512],
                          start=(hc == 0), stop=last,
                          inc=(pl.c["t"] if last else None))
                pl.wait("s", pl.c["t"], pl.c["t"].n)
                pl.gwait("s", "gS")
                pl.op("s", "activation", out=gS, in_=bank(bg), func=AF.Silu,
                      scale=gscT[:, ic:ic + 1], inc=pl.c["s"])
                pl.gset("bank%d" % bg, "s")
                pl.wait("v", pl.c["s"], pl.c["s"].n)
                pl.wait("v", sho[hsl], sho[hsl].n)
                pl.gwait("v", "uS")
                pl.op("v", "tensor_scalar", out=uS, in0=bank(bu),
                      scalar1=uscT[:, ic:ic + 1], scalar2=None, op0=ALU.mult)
                pl.op("v", "tensor_tensor", out=hS[hsl], in0=gS, in1=uS,
                      op=ALU.mult, inc=pl.c["v"])
                pl.gset("bank%d" % bu, "v")
                pl.gset("gS", "v")
                pl.gset("uS", "v")
                pl.wait("s", pl.c["v"], pl.c["v"].n)
                pl.dma("s", hTs[ic * 128:(ic + 1) * 128,
                             ts * 512:(ts + 1) * 512], hS[hsl][:, :], sho[hsl])
            pl.gset("wT", "t")

    def pdown(half):
        a = Ar(pbase)
        wdTall = a.t([128, 32 * D], BF16)
        wdT_v = wdTall.rearrange("p (i c) -> p i c", i=32)
        wdT = [wdTall[:, i * D:(i + 1) * D] for i in range(32)]
        wdi0 = a.t([128, 4096], BF16)
        wdi = [wdi0, wdi0]
        hsl = [a.tl(32, [128, 256], BF16) for _ in range(2)]
        o1t = a.t([128, D], F32)
        pAt = a.t([128, D], F32)
        fin = a.tl(2, [128, D], F32)
        swd = [S("pdw0%d" % half), S("pdw1%d" % half)]
        shl = [S("pdh0%d" % half), S("pdh1%d" % half)]
        so1 = S("pdo1%d" % half)
        spa = S("pdpa%d" % half)
        sfin = [S("pdf0%d" % half), S("pdf1%d" % half)]
        ioff = half * 32
        for hc in range(DC):
            sl = hc % 2
            pl.gwait("y", "wdi")
            pl.dma("y", wdi[sl][:, :],
                   wdN[hc * 128:(hc + 1) * 128, half * 4096:(half + 1) * 4096],
                   swd[sl])
            pl.wait("t", swd[sl])
            for qb in range(4):
                hb = qb % 2
                psT = psT_half(hb)
                pl.gwait("t", "psT%d" % hb)
                for k in range(8):
                    icl = qb * 8 + k
                    pl.op("t", "transpose", psT[:, k * 128:(k + 1) * 128],
                          wdi[sl][:, icl * 128:(icl + 1) * 128], ident,
                          inc=(pl.c["t"] if k == 7 else None))
                pl.wait("v", pl.c["t"], pl.c["t"].n)
                pl.op("v", "tensor_copy",
                      out=wdT_v[:, qb * 8:(qb + 1) * 8,
                                hc * 128:(hc + 1) * 128],
                      in_=psT.rearrange("p (k c) -> p k c", k=8))
                pl.gset("psT%d" % hb, "v")
                if qb == 3:
                    pl.gset("wdi", "t")
        pl.wait("t", pl.c["v"], pl.c["v"].n)
        for tp in range(16):
            hp = tp % 2
            pl.gwait("y", "hsl%d" % hp)
            for icl in range(32):
                pl.dma("y", hsl[hp][icl][:, :],
                       hTs[(ioff + icl) * 128:(ioff + icl + 1) * 128,
                           tp * 256:(tp + 1) * 256], shl[hp])
            for tl in range(2):
                tt = tp * 2 + tl
                fsl = tt % 2
                if half == 1:
                    pl.gwait("y", "o1t")
                    pl.dma("y", o1t[:, :], out1[tt * 128:(tt + 1) * 128, :], so1)
                    pl.gwait("y", "pAt")
                    pl.dma("y", pAt[:, :], pAs[tt * 128:(tt + 1) * 128, :], spa)
                if tl == 0:
                    pl.wait("t", shl[hp])
                for bk in range(4):
                    pl.gwait("t", "bank%d" % bk)
                for icl in range(32):
                    for js in range(4):
                        last = icl == 31
                        pl.op("t", "matmul", bank(js),
                              hsl[hp][icl][:, tl * 128:(tl + 1) * 128],
                              wdT[icl][:, js * 512:(js + 1) * 512],
                              start=(icl == 0), stop=last,
                              inc=(pl.c["t"] if (last and js == 3) else None))
                if tl == 1:
                    pl.gset("hsl%d" % hp, "t")
                if half == 0:
                    pl.wait("v", pl.c["t"], pl.c["t"].n)
                    pl.wait("v", sfin[fsl], sfin[fsl].n)
                    for js in range(4):
                        pl.op("v", "tensor_copy",
                              out=fin[fsl][:, js * 512:(js + 1) * 512],
                              in_=bank(js),
                              inc=(pl.c["v"] if js == 3 else None))
                    for bk in range(4):
                        pl.gset("bank%d" % bk, "v")
                    pl.wait("s", pl.c["v"], pl.c["v"].n)
                    pl.dma("s", pAs[tt * 128:(tt + 1) * 128, :], fin[fsl][:, :],
                           sfin[fsl])
                else:
                    pl.wait("v", so1)
                    pl.wait("v", spa)
                    pl.wait("v", sfin[fsl], sfin[fsl].n)
                    pl.op("v", "tensor_tensor", out=fin[fsl], in0=o1t, in1=pAt,
                          op=ALU.add)
                    pl.gset("o1t", "v")
                    pl.gset("pAt", "v")
                    pl.wait("v", pl.c["t"], pl.c["t"].n)
                    pl.op("v", "tensor_tensor", out=fin[fsl], in0=fin[fsl],
                          in1=ps7[:, 0:2048], op=ALU.add, inc=pl.c["v"])
                    for bk in range(4):
                        pl.gset("bank%d" % bk, "v")
                    pl.wait("s", pl.c["v"], pl.c["v"].n)
                    pl.dma("s", y_h[tt * 128:(tt + 1) * 128, :], fin[fsl][:, :],
                           sfin[fsl])

    phases = [
        ("p01", p01),
        ("p0d1", p0d1),
        ("p0d2", p0d2),
        ("p2a", lambda: proj_phase(wqN, True, qTs, 1.0 / float(np.sqrt(HD)),
                                   "pa", conv="g")),
        ("p2b", lambda: proj_phase(wkN, True, kTs, 1.0, "pb", conv="u")),
        ("p2c", lambda: proj_phase(wkN, False, qmTs, 1.0 / float(np.sqrt(D)),
                                   "pc", conv="d")),
        ("p2d", p2d),
        ("p3", p3),
        ("p4", p4),
        ("p5", p5),
        ("p6", p6),
        ("p7", lambda: pdown(0)),
        ("p8", lambda: pdown(1)),
    ]
    upto = _PHASES_UPTO if _PHASES_UPTO is not None else len(phases)
    for name, fn in phases[:upto]:
        if _PHASE_SELECT is not None and name not in _PHASE_SELECT:
            continue
        fn()
        pl.boundary()

    for pi in range(len(pl.marks) - 1):
        with nc.Block() as block:
            emap = {"t": block.tensor, "v": block.vector, "s": block.scalar,
                    "g": block.gpsimd, "y": block.sync}
            for e in ENGS:
                lo = pl.marks[pi][e]
                hi = pl.marks[pi + 1][e]
                if hi == lo:
                    continue
                def body(b, e=e, lo=lo, hi=hi):
                    for f in pl.q[e][lo:hi]:
                        f(b)
                emap[e](body)
    return nc


def kernel(**inputs):
    global _LAST_RESULTS
    nc = _build()
    xs = inputs["L0_attn_hidden_states"]
    base = {
        "wq": inputs["L0_attn_wq"],
        "wk": inputs["L0_attn_wk"],
        "wv": inputs["L0_attn_wv"],
        "wo": inputs["L0_attn_wo"],
        "gw": inputs["L0_attn_gate_w"],
        "gb": inputs["L0_attn_gate_b"],
        "mem": inputs["L0_attn_mem_bank"],
        "fg": inputs["L0_ffn_gate_w"],
        "fgs": inputs["L0_ffn_gate_scale"],
        "fu": inputs["L0_ffn_up_w"],
        "fus": inputs["L0_ffn_up_scale"],
        "fd": inputs["L0_ffn_down_w"],
        "fds": inputs["L0_ffn_down_scale"],
    }
    in_maps = [dict(base, x=np.ascontiguousarray(xs[b])) for b in range(NCORE)]
    res = run_bass_kernel_spmd(nc, in_maps, core_ids=list(range(NCORE)))
    _LAST_RESULTS = res.results
    out = np.stack([np.asarray(r["y"], dtype=np.float32)
                    for r in res.results], axis=0)
    return out


# version 07 (final)
# speedup vs baseline: 1.0146x; 1.0146x over previous
import numpy as np
import concourse.bass as bass
import concourse.mybir as mybir
from concourse.bass_utils import run_bass_kernel_spmd

F32 = mybir.dt.float32
BF16 = mybir.dt.bfloat16
I32 = mybir.dt.int32
AF = mybir.ActivationFunctionType
ALU = mybir.AluOpType
AX = mybir.AxisListType.X

NEG = -1.0e30
T, D, H, HD, W, M, I = 4096, 2048, 16, 128, 256, 4096, 8192
TT = T // 128
DC = D // 128
MC = M // 128
IC = I // 128
NCORE = 8

_DBG_OUTS = set()
_PHASES_UPTO = None
_PHASE_SELECT = None
_P0_PARTS = {"masks", "gate", "casts", "scales", "conv_gu", "conv_d"}
_LAST_RESULTS = None

ENGS = ("t", "v", "s", "g", "y")


class _Sem:
    __slots__ = ("h", "n")

    def __init__(self, nc, name):
        self.h = nc.alloc_semaphore(name)
        self.n = 0


class _PL:

    def __init__(self, nc):
        self.nc = nc
        self.q = {e: [] for e in ENGS}
        self.c = {e: _Sem(nc, "c" + e) for e in ENGS}
        self.dirty = {e: set() for e in ENGS}
        self.guards = {}
        self._sems = {}
        self.marks = [{e: 0 for e in ENGS}]

    def sem(self, name):
        if name not in self._sems:
            self._sems[name] = _Sem(self.nc, name)
        return self._sems[name]

    def wait(self, e, sem, val=None):
        v = sem.n if val is None else val
        if v <= 0:
            return
        self.q[e].append(lambda b, s=sem.h, v=v: b.wait_ge(s, v))

    def inc(self, e, sem, n=1):
        sem.n += n
        self.q[e].append(lambda b, s=sem.h, n=n: b.sem_inc(s, n))

    def dma(self, e, out, in_, sem, transpose=False):
        sem.n += 16
        self.dirty[e].add(sem)

        def f(b, o=out, i=in_, s=sem.h, tr=transpose):
            if tr:
                b.dma_start(out=o, in_=i, transpose=True).then_inc(s, 16)
            else:
                b.dma_start(out=o, in_=i).then_inc(s, 16)

        self.q[e].append(f)

    def op(self, e, meth, *a, inc=None, **kw):
        auto = e in ("v", "s", "g")
        if auto:
            self.wait(e, self.c[e], self.c[e].n)
            assert inc is None or inc is self.c[e]
            inc = self.c[e]
        if inc is not None:
            inc.n += 1

        def f(b, meth=meth, a=a, kw=kw, inc=inc):
            r = getattr(b, meth)(*a, **kw)
            if inc is not None:
                r.then_inc(inc.h, 1)

        self.q[e].append(f)

    def gset(self, key, eng):
        self.guards[key] = (eng, self.c[eng].n)

    def gwait(self, e, key):
        if key in self.guards:
            eng, val = self.guards[key]
            if eng == e:
                return
            self.wait(e, self.c[eng], val)

    def boundary(self):
        for e in ENGS:
            for s in self.dirty[e]:
                self.wait(e, s)
            self.dirty[e] = set()
        self.guards = {}
        self.marks.append({e: len(self.q[e]) for e in ENGS})


def _build():
    nc = bass.Bass("TRN2", target_bir_lowering=False)

    def din(name, shape, dt):
        return nc.dram_tensor(name, shape, dt, kind="ExternalInput").ap()

    def dscr(name, shape, dt):
        kind = "ExternalOutput" if name in _DBG_OUTS else "Internal"
        return nc.dram_tensor(name, shape, dt, kind=kind).ap()

    x_h = din("x", [T, D], F32)
    wq_h = din("wq", [D, D], F32)
    wk_h = din("wk", [D, D], F32)
    wv_h = din("wv", [D, D], F32)
    wo_h = din("wo", [D, D], F32)
    gw_h = din("gw", [1, D], F32)
    gb_h = din("gb", [1], F32)
    mem_h = din("mem", [M, D], F32)
    fg_h = din("fg", [I, D], I32)
    fgs_h = din("fgs", [I], F32)
    fu_h = din("fu", [I, D], I32)
    fus_h = din("fus", [I], F32)
    fd_h = din("fd", [D, I], I32)
    fds_h = din("fds", [D], F32)
    y_h = nc.dram_tensor("y", [T, D], F32, kind="ExternalOutput").ap()

    wqN = dscr("wqN", [D, D], BF16)
    wkN = dscr("wkN", [D, D], BF16)
    wvN = dscr("wvN", [D, D], BF16)
    woN = dscr("woN", [D, D], BF16)
    memN = dscr("memN", [M, D], BF16)
    wgN = dscr("wgN", [I, D], BF16)
    wuN = dscr("wuN", [I, D], BF16)
    wdN = dscr("wdN", [D, I], BF16)
    vmT = dscr("vmT", [D, M], BF16)
    vo = dscr("vo", [M, D], BF16)
    xlnT = dscr("xlnT", [D, T], BF16)
    qTs = dscr("qTs", [D, T], BF16)
    kTs = dscr("kTs", [D, T], BF16)
    qmTs = dscr("qmTs", [D, T], BF16)
    vts = dscr("vts", [T, D], BF16)
    oloc = dscr("oloc", [T, D], F32)
    sBs = dscr("sBs", [T, M], BF16)
    out1 = dscr("out1", [T, D], F32)
    gateRs = dscr("gateRs", [128, D], F32)
    yTs = dscr("yTs", [D, T], BF16)
    hTs = dscr("hTs", [I, T], BF16)
    pAs = dscr("pAs", [T, D], F32)

    _uid = [0]

    def sb_at(off, shape, dt):
        _uid[0] += 1
        assert off % 32 == 0
        return nc.alloc_sbuf_tensor_at(
            "sb%d" % _uid[0], shape, dt, offset=off).ap()

    base0 = (nc.sbuf_base + 31) // 32 * 32
    cap = nc.sbuf_top

    class Ar:
        def __init__(self, start):
            self.off = start

        def t(self, shape, dt):
            nbytes = 1
            for s in shape[1:]:
                nbytes *= s
            nbytes *= {F32: 4, BF16: 2, I32: 4}[dt]
            nbytes = (nbytes + 31) // 32 * 32
            r = sb_at(self.off, shape, dt)
            self.off += nbytes
            assert self.off <= cap, f"sbuf overflow {self.off} > {cap}"
            return r

        def tl(self, n, shape, dt):
            return [self.t(shape, dt) for _ in range(n)]

    per = Ar(base0)
    ident = per.t([128, 128], BF16)
    gbT = per.t([128, 1], F32)
    gscT = per.t([128, IC], F32)
    uscT = per.t([128, IC], F32)
    dscT = per.t([128, DC], F32)
    bypAll = per.t([128, TT], F32)
    l8All = per.t([128, TT], F32)
    c_eps = per.t([128, 1], F32)
    c_invD = per.t([128, 1], F32)
    c_sHD = per.t([128, 1], F32)
    c_sD = per.t([128, 1], F32)
    pbase = (per.off + 31) // 32 * 32

    ps7 = nc.alloc_psum_tensor("ps7", [128, 3072], F32).ap()
    psTf = nc.alloc_psum_tensor("psT", [128, 2048], BF16).ap()

    def bank(k, w=512, o=0):
        return ps7[:, 512 * k + o:512 * k + o + w]

    def psT_half(j):
        return psTf[:, 1024 * j:1024 * j + 1024]

    pl = _PL(nc)
    S = pl.sem

    def conv_gu(wsrc, wdst, ar, tag):
        bi = ar.tl(2, [128, D], I32)
        bo = ar.tl(2, [128, D], BF16)
        sbi = [S(tag + "i0"), S(tag + "i1")]
        sbo = [S(tag + "o0"), S(tag + "o1")]
        state = [0]

        def step():
            ic = state[0]
            if ic >= IC:
                return
            state[0] += 1
            sl = ic % 2
            pl.wait("y", sbo[sl])
            pl.dma("y", bi[sl][:, :], wsrc[ic * 128:(ic + 1) * 128, :], sbi[sl])
            pl.wait("v", sbi[sl])
            pl.op("v", "tensor_copy", out=bo[sl], in_=bi[sl])
            pl.wait("s", pl.c["v"], pl.c["v"].n)
            pl.dma("s", wdst[ic * 128:(ic + 1) * 128, :], bo[sl][:, :], sbo[sl])
        return step

    def conv_d(ar):
        di = ar.tl(2, [128, 4096], I32)
        do = ar.tl(2, [128, 4096], BF16)
        sdi = [S("cdi0"), S("cdi1")]
        sdo = [S("cdo0"), S("cdo1")]
        state = [0]

        def step():
            k = state[0]
            if k >= DC * 2:
                return
            state[0] += 1
            hc, half = k // 2, k % 2
            sl = k % 2
            pl.wait("y", sdo[sl])
            pl.dma("y", di[sl][:, :],
                   fd_h[hc * 128:(hc + 1) * 128,
                        half * 4096:(half + 1) * 4096], sdi[sl])
            pl.wait("v", sdi[sl])
            pl.op("v", "tensor_scalar", out=do[sl], in0=di[sl],
                  scalar1=-128.0, scalar2=dscT[:, hc:hc + 1],
                  op0=ALU.add, op1=ALU.mult)
            pl.wait("s", pl.c["v"], pl.c["v"].n)
            pl.dma("s", wdN[hc * 128:(hc + 1) * 128,
                         half * 4096:(half + 1) * 4096], do[sl][:, :],
                   sdo[sl])
        return step

    def p01():
        a = Ar(pbase)
        pl.op("v", "memset", c_eps, 1e-5)
        pl.op("v", "memset", c_invD, 1.0 / D)
        pl.op("v", "memset", c_sHD, 1.0 / float(np.sqrt(HD)))
        pl.op("v", "memset", c_sD, 1.0 / float(np.sqrt(D)))
        ones1 = a.t([1, 128], F32)
        grow = a.t([1, D], F32)
        gbrow = a.t([1, 1], F32)
        gateR0 = a.t([128, D], F32)
        sq = S("p0q")
        sg = S("p0g")
        pl.op("g", "memset", ident, 1.0, inc=pl.c["g"])
        pl.wait("g", pl.c["g"], pl.c["g"].n)
        pl.op("g", "affine_select", out=ident, in_=ident, pattern=[[1, 128]],
              compare_op=ALU.is_equal, fill=0.0, base=0, channel_multiplier=-1,
              inc=pl.c["g"])
        pl.dma("g", wqN[:, :], wq_h[:, :], sg)
        pl.dma("g", wkN[:, :], wk_h[:, :], sg)
        pl.dma("g", wvN[:, :], wv_h[:, :], sg)
        pl.dma("g", woN[:, :], wo_h[:, :], sg)
        pl.dma("g", memN[:, :], mem_h[:, :], sg)
        pl.dma("y", grow[:, :], gw_h[:, :], sq)
        pl.dma("y", gbrow[:, :], gb_h.rearrange("(a b) -> a b", b=1), sq)
        for c in range(IC):
            pl.dma("y", gscT[:, c:c + 1],
                   fgs_h[c * 128:(c + 1) * 128].rearrange("(p o) -> p o", o=1), sq)
            pl.dma("y", uscT[:, c:c + 1],
                   fus_h[c * 128:(c + 1) * 128].rearrange("(p o) -> p o", o=1), sq)
        for c in range(DC):
            pl.dma("y", dscT[:, c:c + 1],
                   fds_h[c * 128:(c + 1) * 128].rearrange("(p o) -> p o", o=1), sq)
        pl.op("v", "memset", ones1, 1.0, inc=pl.c["v"])
        pl.wait("t", pl.c["v"], pl.c["v"].n)
        pl.wait("t", sq)
        pl.wait("t", pl.c["g"], pl.c["g"].n)
        pl.op("t", "matmul", bank(0, 1), ones1[0:1, :], gbrow[0:1, :],
              start=True, stop=True, inc=pl.c["t"])
        pl.wait("v", pl.c["t"], pl.c["t"].n)
        pl.op("v", "tensor_copy", out=gbT, in_=bank(0, 1))
        pl.gset("bank0", "v")
        for jsb in range(4):
            pl.gwait("t", "bank1")
            pl.op("t", "matmul", bank(1), ones1[0:1, :],
                  grow[0:1, jsb * 512:(jsb + 1) * 512], start=True,
                  stop=True, inc=pl.c["t"])
            pl.wait("v", pl.c["t"], pl.c["t"].n)
            pl.op("v", "tensor_copy",
                  out=gateR0[:, jsb * 512:(jsb + 1) * 512],
                  in_=bank(1))
            pl.gset("bank1", "v")
        pl.wait("s", pl.c["v"], pl.c["v"].n)
        pl.dma("s", gateRs[:, :], gateR0[:, :], S("p0gr"))
        xt = a.tl(2, [128, D], F32)
        sqd = a.t([128, D], F32)
        xln = a.t([128, D], BF16)
        slab = a.t([128, DC * 512], BF16)
        small = a.tl(5, [128, 1], F32)
        sx = [S("lnx0"), S("lnx1")]
        sst = S("lnst")
        slab_v = slab.rearrange("p (dc tl c) -> p dc tl c", dc=DC, tl=4)
        for tt in range(TT):
            sl = tt % 2
            pl.gwait("y", "xt%d" % sl)
            pl.dma("y", xt[sl][:, :], x_h[tt * 128:(tt + 1) * 128, :], sx[sl])
            pl.wait("v", sx[sl])
            pl.guards.pop("cent", None)
            ln_tile((small[0], small[1], small[2], small[3], small[4], sqd),
                    xt[sl], xt[sl], xln)
            pl.gset("xt%d" % sl, "v")
            tl = tt % 4
            for hb in range(2):
                psT = psT_half(hb)
                psT_v = psT.rearrange("p (k o c) -> p k o c", k=8, o=1)
                pl.gwait("t", "psT%d" % hb)
                pl.wait("t", pl.c["v"], pl.c["v"].n)
                if tl == 0 and hb == 0:
                    pl.wait("t", sst, sst.n)
                for k in range(8):
                    pl.op("t", "transpose", psT[:, k * 128:(k + 1) * 128],
                          xln[:, (hb * 8 + k) * 128:(hb * 8 + k + 1) * 128],
                          ident, inc=(pl.c["t"] if k == 7 else None))
                pl.wait("v", pl.c["t"], pl.c["t"].n)
                pl.op("v", "tensor_copy",
                      out=slab_v[:, hb * 8:hb * 8 + 8, tl:tl + 1, :], in_=psT_v)
                pl.gset("psT%d" % hb, "v")
            if tl == 3:
                pl.wait("s", pl.c["v"], pl.c["v"].n)
                g = tt // 4
                for dc in range(DC):
                    pl.dma("s", xlnT[dc * 128:(dc + 1) * 128,
                                   g * 512:(g + 1) * 512],
                           slab[:, dc * 512:(dc + 1) * 512], sst)

    def p0d1():
        a = Ar(pbase)
        wvT = a.tl(DC, [128, D], BF16)
        memTh = a.tl(DC, [128, 2048], BF16)
        stg = a.tl(2, [128, 512], BF16)
        sw = S("d1w")
        sm = S("d1m")
        sst = [S("d1s0"), S("d1s1")]
        for ec in range(DC):
            pl.dma("s", wvT[ec][:, :], wvN[:, ec * 128:(ec + 1) * 128], sw,
                   transpose=True)
        for mh in range(2):
            if mh == 1:
                pl.gwait("s", "memTh")
            for ec in range(DC):
                pl.dma("s", memTh[ec][:, :],
                       memN[mh * 2048:(mh + 1) * 2048, ec * 128:(ec + 1) * 128],
                       sm, transpose=True)
            pl.wait("t", sw)
            pl.wait("t", sm)
            for jc in range(DC):
                for ms in range(4):
                    bk = ms
                    pl.gwait("t", "bank%d" % bk)
                    for ec in range(DC):
                        last = ec == DC - 1
                        pl.op("t", "matmul", bank(bk),
                              wvT[ec][:, jc * 128:(jc + 1) * 128],
                              memTh[ec][:, ms * 512:(ms + 1) * 512],
                              start=(ec == 0), stop=last,
                              inc=(pl.c["t"] if last else None))
                    sl = ms % 2
                    pl.wait("v", pl.c["t"], pl.c["t"].n)
                    pl.wait("v", sst[sl])
                    pl.op("v", "tensor_copy", out=stg[sl], in_=bank(bk),
                          inc=pl.c["v"])
                    pl.gset("bank%d" % bk, "v")
                    pl.wait("s", pl.c["v"], pl.c["v"].n)
                    pl.dma("s", vmT[jc * 128:(jc + 1) * 128,
                                  mh * 2048 + ms * 512:mh * 2048 + (ms + 1) * 512],
                           stg[sl][:, :], sst[sl])
            pl.gset("memTh", "t")

    def p0d2():
        a = Ar(pbase)
        woT = a.tl(DC, [128, D], BF16)
        slab = a.tl(DC, [128, 512], BF16)
        stg = a.tl(2, [128, D], BF16)
        sw = S("d2w")
        sm = S("d2m")
        sst = [S("d2s0"), S("d2s1")]
        for vc in range(DC):
            pl.dma("s", woT[vc][:, :], woN[:, vc * 128:(vc + 1) * 128], sw,
                   transpose=True)
        pl.wait("t", sw)
        for mg in range(8):
            pl.gwait("y", "d2slab")
            for vc in range(DC):
                pl.dma("y", slab[vc][:, :],
                       vmT[vc * 128:(vc + 1) * 128, mg * 512:(mg + 1) * 512], sm)
            pl.wait("t", sm)
            for ml in range(4):
                sl = ml % 2
                pl.gwait("t", "d2stg%d" % sl)
                pl.wait("t", sst[sl], sst[sl].n)
                for bk in range(4):
                    pl.gwait("t", "bank%d" % bk)
                for vc in range(DC):
                    for os in range(4):
                        last = vc == DC - 1
                        pl.op("t", "matmul", bank(os),
                              slab[vc][:, ml * 128:(ml + 1) * 128],
                              woT[vc][:, os * 512:(os + 1) * 512],
                              start=(vc == 0), stop=last,
                              inc=(pl.c["t"] if (last and os == 3) else None))
                pl.wait("v", pl.c["t"], pl.c["t"].n)
                for os in range(4):
                    pl.op("v", "tensor_copy", out=stg[sl][:, os * 512:(os + 1) * 512],
                          in_=bank(os), inc=(pl.c["v"] if os == 3 else None))
                for bk in range(4):
                    pl.gset("bank%d" % bk, "v")
                pl.gset("d2stg%d" % sl, "v")
                pl.wait("s", pl.c["v"], pl.c["v"].n)
                pl.dma("s", vo[(mg * 4 + ml) * 128:(mg * 4 + ml + 1) * 128, :],
                       stg[sl][:, :], sst[sl])
            pl.gset("d2slab", "t")

    def ln_tile(a_bufs, src_sbuf, cent, dst_bf16):
        sums, mean, vs, sdv, rstd, sqd = a_bufs
        pl.op("v", "reduce_sum", out=sums, in_=src_sbuf, axis=AX)
        pl.op("v", "tensor_scalar", out=mean, in0=sums, scalar1=1.0 / D,
              scalar2=None, op0=ALU.mult)
        pl.gwait("v", "cent")
        pl.op("v", "tensor_scalar", out=cent, in0=src_sbuf, scalar1=mean,
              scalar2=None, op0=ALU.subtract)
        pl.wait("s", pl.c["v"], pl.c["v"].n)
        pl.op("s", "activation", out=sqd, in_=cent, func=AF.Square,
              accum_out=vs)
        pl.op("s", "activation", out=sdv, in_=vs, func=AF.Sqrt,
              scale=c_invD[:, 0:1], bias=c_eps[:, 0:1])
        pl.wait("v", pl.c["s"], pl.c["s"].n)
        pl.op("v", "reciprocal", out=rstd, in_=sdv)
        pl.gwait("v", "lnout")
        pl.op("v", "tensor_scalar", out=dst_bf16, in0=cent, scalar1=rstd,
              scalar2=None, op0=ALU.mult)

    def proj_phase(wsrc, transposed, dst, scale, tag, conv=None):
        scale_ap = 1.0
        if scale != 1.0:
            scale_ap = {1.0 / float(np.sqrt(HD)): c_sHD,
                        1.0 / float(np.sqrt(D)): c_sD}[scale][:, 0:1]
        a = Ar(pbase)
        wt = a.tl(DC, [128, D], BF16)
        xs = [a.tl(DC, [128, 512], BF16) for _ in range(2)]
        stg = a.tl(4, [128, 512], BF16)
        cstep = None
        if conv == "g":
            cstep = conv_gu(fg_h, wgN, a, "cvg")
        elif conv == "u":
            cstep = conv_gu(fu_h, wuN, a, "cvu")
        elif conv == "d":
            cstep = conv_d(a)
        sw = S(tag + "w")
        sxs = [S(tag + "x0"), S(tag + "x1")]
        sst = [S(tag + "s%d" % i) for i in range(4)]
        for dc in range(DC):
            if transposed:
                pl.dma("s", wt[dc][:, :], wsrc[:, dc * 128:(dc + 1) * 128], sw,
                       transpose=True)
            else:
                pl.dma("s", wt[dc][:, :], wsrc[dc * 128:(dc + 1) * 128, :], sw)
        pl.wait("t", sw)
        for ts in range(8):
            sl = ts % 2
            pl.gwait("y", tag + "xs%d" % sl)
            for dc in range(DC):
                pl.dma("y", xs[sl][dc][:, :],
                       xlnT[dc * 128:(dc + 1) * 128, ts * 512:(ts + 1) * 512],
                       sxs[sl])
            pl.wait("t", sxs[sl])
            for jc in range(DC):
                bk = jc % 4
                ee = "v" if jc % 2 == 0 else "s"
                pl.gwait("t", "bank%d" % bk)
                pl.wait("t", sst[bk], sst[bk].n)
                for dc in range(DC):
                    last = dc == DC - 1
                    pl.op("t", "matmul", bank(bk),
                          wt[dc][:, jc * 128:(jc + 1) * 128], xs[sl][dc][:, :],
                          start=(dc == 0), stop=last,
                          inc=(pl.c["t"] if last else None))
                pl.wait(ee, pl.c["t"], pl.c["t"].n)
                if ee == "v":
                    pl.op("v", "tensor_scalar", out=stg[bk], in0=bank(bk),
                          scalar1=scale, scalar2=None, op0=ALU.mult,
                          inc=pl.c["v"])
                else:
                    pl.op("s", "activation", out=stg[bk], in_=bank(bk),
                          func=AF.Copy, scale=scale_ap, inc=pl.c["s"])
                pl.gset("bank%d" % bk, ee)
                pl.wait("s", pl.c[ee], pl.c[ee].n)
                pl.dma("s", dst[jc * 128:(jc + 1) * 128, ts * 512:(ts + 1) * 512],
                       stg[bk][:, :], sst[bk])
                if cstep is not None and (jc % 2 == 1):
                    cstep()
            pl.gset(tag + "xs%d" % sl, "t")
        if cstep is not None:
            for _ in range(2 * IC):
                cstep()

    def p2d():
        a = Ar(pbase)
        wvT = a.tl(DC, [128, D], BF16)
        xs = [a.tl(DC, [128, 512], BF16) for _ in range(2)]
        stg = a.tl(2, [128, D], BF16)
        sw = S("vtw")
        sxs = [S("vtx0"), S("vtx1")]
        sst = [S("vts0"), S("vts1")]
        for dc in range(DC):
            pl.dma("s", wvT[dc][:, :], wvN[:, dc * 128:(dc + 1) * 128], sw,
                   transpose=True)
        pl.wait("t", sw)
        for g in range(8):
            sl = g % 2
            pl.gwait("y", "vtxs%d" % sl)
            for dc in range(DC):
                pl.dma("y", xs[sl][dc][:, :],
                       xlnT[dc * 128:(dc + 1) * 128, g * 512:(g + 1) * 512],
                       sxs[sl])
            pl.wait("t", sxs[sl])
            for tl in range(4):
                tt = g * 4 + tl
                ssl = tt % 2
                pl.wait("t", sst[ssl], sst[ssl].n)
                pl.gwait("t", "vtstg%d" % ssl)
                for bk in range(4):
                    pl.gwait("t", "bank%d" % bk)
                for dc in range(DC):
                    for js in range(4):
                        last = dc == DC - 1
                        pl.op("t", "matmul", bank(js),
                              xs[sl][dc][:, tl * 128:(tl + 1) * 128],
                              wvT[dc][:, js * 512:(js + 1) * 512],
                              start=(dc == 0), stop=last,
                              inc=(pl.c["t"] if (last and js == 3) else None))
                pl.wait("v", pl.c["t"], pl.c["t"].n)
                for js in range(4):
                    pl.op("v", "tensor_copy",
                          out=stg[ssl][:, js * 512:(js + 1) * 512], in_=bank(js),
                          inc=(pl.c["v"] if js == 3 else None))
                for bk in range(4):
                    pl.gset("bank%d" % bk, "v")
                pl.gset("vtstg%d" % ssl, "v")
                pl.wait("s", pl.c["v"], pl.c["v"].n)
                pl.dma("s", vts[tt * 128:(tt + 1) * 128, :], stg[ssl][:, :],
                       sst[ssl])
            pl.gset("vtxs%d" % sl, "t")

    def p3():
        a = Ar(pbase)
        woT = a.tl(DC, [128, D], BF16)
        gateR = a.t([128, D], F32)
        maskG = a.t([128, 384], F32)
        mask1 = a.t([128, 256], F32)
        mask0 = a.t([128, 128], F32)
        m4G = a.t([128, 4 * 384], F32)
        iv = a.t([128, 384], F32)
        tmp = a.t([128, 384], F32)
        qsl = a.tl(H, [128, 512], BF16)
        ksl = a.tl(H, [128, 768], BF16)
        vsl = a.tl(6, [128, D], BF16)
        _ssb_off = a.off
        ssb4d = a.tl(2, [128, 4 * 384], F32)
        pE4d = a.tl(2, [128, 4 * 384], F32)
        Pb4d = a.tl(2, [128, 4 * 384], BF16)
        PT = a.t([128, 4 * 384], BF16)
        ctxq = a.tl(4, [128, 512], BF16)
        ostg = a.tl(2, [128, D], F32)
        negm4d = a.tl(2, [128, 4], F32)
        lsum4d = a.tl(2, [128, 4], F32)
        rl4d = a.tl(2, [128, 4], F32)
        bl = a.t([128, 1], F32)
        junk = sb_at(_ssb_off, [128, D], F32)
        sw = S("p3w")
        sq = S("p3q")
        sk = S("p3k")
        sv = S("p3v")
        sst = [S("p3s0"), S("p3s1")]
        pl.op("g", "iota", iv, [[1, 384]], base=0, channel_multiplier=-1,
              allow_small_or_imprecise_dtypes=True, inc=pl.c["g"])
        pl.wait("v", pl.c["g"], pl.c["g"].n)
        pl.op("v", "tensor_scalar", out=tmp, in0=iv, scalar1=1.0,
              scalar2=None, op0=ALU.is_ge)
        pl.op("v", "tensor_scalar", out=maskG, in0=iv, scalar1=256.0,
              scalar2=None, op0=ALU.is_le)
        pl.op("v", "tensor_tensor", out=maskG, in0=maskG, in1=tmp, op=ALU.mult)
        pl.op("v", "tensor_scalar", out=maskG, in0=maskG, scalar1=-1.0,
              scalar2=-NEG, op0=ALU.add, op1=ALU.mult)
        pl.op("v", "tensor_scalar", out=mask1, in0=iv[:, 0:256], scalar1=128.0,
              scalar2=None, op0=ALU.is_le)
        pl.op("v", "tensor_scalar", out=mask1, in0=mask1, scalar1=-1.0,
              scalar2=-NEG, op0=ALU.add, op1=ALU.mult)
        pl.op("v", "tensor_scalar", out=mask0, in0=iv[:, 0:128], scalar1=0.0,
              scalar2=None, op0=ALU.is_le)
        pl.op("v", "tensor_scalar", out=mask0, in0=mask0, scalar1=-1.0,
              scalar2=-NEG, op0=ALU.add, op1=ALU.mult)
        for j in range(4):
            pl.op("v", "tensor_copy", out=m4G[:, j * 384:(j + 1) * 384],
                  in_=maskG)
        for dc in range(DC):
            pl.dma("s", woT[dc][:, :], woN[:, dc * 128:(dc + 1) * 128], sw,
                   transpose=True)
        pl.dma("y", gateR[:, :], gateRs[:, :], S("p3gr"))
        pl.wait("v", S("p3gr"))
        pl.wait("t", sw)

        for g in range(8):
            for h in range(H):
                pl.gwait("y", "p3qk")
                pl.dma("y", qsl[h][:, :],
                       qTs[h * 128:(h + 1) * 128, g * 512:(g + 1) * 512], sq)
                lo = g * 512 - 256
                off = 0
                if lo < 0:
                    off = -lo
                    lo = 0
                pl.dma("y", ksl[h][:, off:768],
                       kTs[h * 128:(h + 1) * 128, lo:g * 512 + 512], sk)
            for tl4 in range(4):
                tt = g * 4 + tl4
                pl.gwait("y", "p3qk")
                pl.dma("y", vsl[tt % 6][:, :],
                       vts[tt * 128:(tt + 1) * 128, :], sv)
            pl.wait("t", sq)
            pl.wait("t", sk)
            pl.wait("t", sv)
            for tl4 in range(4):
                tt = g * 4 + tl4
                kcc = min(tt + 1, 3)
                fc = tt - kcc + 1
                m4 = m4G
                msk1h = maskG if tt >= 2 else (mask1 if tt == 1 else mask0)
                wmask = kcc * 128
                pb_done = {}

                def scores_quad(qd):
                    for j in range(4):
                        h = qd * 4 + j
                        pl.gwait("t", "sbank%d" % j)
                        for kc in range(kcc):
                            slabc = (fc + kc) - (g * 4 - 2)
                            pl.op("t", "matmul", bank(j, 128, o=kc * 128),
                                  qsl[h][:, tl4 * 128:(tl4 + 1) * 128],
                                  ksl[h][:, slabc * 128:(slabc + 1) * 128],
                                  start=True, stop=True,
                                  inc=(pl.c["t"] if (kc == kcc - 1 and j == 3)
                                       else None))
                    return pl.c["t"].n

                def backend_quad(qd):
                    qp = qd % 2
                    Pb4 = Pb4d[qp]
                    pl.wait("t", pl.c["v"], pb_done[qd])
                    for hb2 in range(2):
                        psT = psT_half(hb2)
                        pl.gwait("t", "psT%d" % hb2)
                        for jj in range(2):
                            j = hb2 * 2 + jj
                            for kc in range(kcc):
                                lastt = (jj == 1 and kc == kcc - 1)
                                pl.op("t", "transpose",
                                      psT[:, (jj * kcc + kc) * 128:
                                          (jj * kcc + kc + 1) * 128],
                                      Pb4[:, j * 384 + kc * 128:
                                          j * 384 + (kc + 1) * 128],
                                      ident,
                                      inc=(pl.c["t"] if lastt else None))
                        pl.wait("v", pl.c["t"], pl.c["t"].n)
                        pl.gwait("v", "PTh%d" % hb2)
                        pl.op("v", "tensor_copy",
                              out=PT[:, hb2 * 768:hb2 * 768 + 2 * wmask],
                              in_=psT[:, 0:2 * wmask])
                        pl.gset("psT%d" % hb2, "v")
                    pl.gset("Pb4_%d" % qp, "t")
                    pl.wait("t", pl.c["v"], pl.c["v"].n)
                    pl.gwait("t", "bank4")
                    for j in range(4):
                        h = qd * 4 + j
                        hb2, jj = j // 2, j % 2
                        for kc in range(kcc):
                            pl.op("t", "matmul", bank(4, 128, o=j * 128),
                                  vsl[(fc + kc) % 6][:, h * 128:(h + 1) * 128],
                                  PT[:, hb2 * 768 + (jj * kcc + kc) * 128:
                                      hb2 * 768 + (jj * kcc + kc + 1) * 128],
                                  start=(kc == 0), stop=(kc == kcc - 1),
                                  inc=(pl.c["t"] if (j == 3 and kc == kcc - 1)
                                       else None))
                    pl.gset("PTh0", "t")
                    pl.gset("PTh1", "t")
                    pl.wait("v", pl.c["t"], pl.c["t"].n)
                    pl.gwait("v", "ctxq%d" % qd)
                    pl.op("v", "tensor_copy", out=ctxq[qd], in_=bank(4))
                    pl.gset("bank4", "v")

                def front1(qd):
                    qp = qd % 2
                    ps_v = ps7[:, 0:2048].rearrange(
                        "p (h c) -> p h c", h=4)[:, :, 0:wmask]
                    ssb_v = ssb4d[qp].rearrange(
                        "p (h c) -> p h c", h=4)[:, :, 0:wmask]
                    m4_v = m4.rearrange(
                        "p (h c) -> p h c", h=4)[:, :, 0:wmask]
                    pl.wait("v", pl.c["t"], sd[qd])
                    pl.gwait("v", "ssb4_%d" % qp)
                    pl.gwait("v", "negm4_%d" % qp)
                    if tt >= 2:
                        pl.op("v", "tensor_tensor", out=ssb_v, in0=ps_v,
                              in1=m4_v, op=ALU.add)
                    else:
                        for j in range(4):
                            pl.op("v", "tensor_tensor",
                                  out=ssb4d[qp][:, j * 384:j * 384 + wmask],
                                  in0=bank(j, wmask),
                                  in1=msk1h[:, 0:wmask], op=ALU.add)
                    pl.op("v", "reduce_max", out=negm4d[qp], in_=ssb_v,
                          axis=AX, negate=True)
                    add_done[qd] = pl.c["v"].n
                    for j in range(4):
                        pl.gset("sbank%d" % j, "v")

                def front2(qd):
                    qp = qd % 2
                    pl.wait("s", pl.c["v"], add_done[qd])
                    pl.gwait("s", "lsum4_%d" % qp)
                    for j in range(4):
                        pl.op("s", "activation",
                              out=pE4d[qp][:, j * 384:j * 384 + wmask],
                              in_=ssb4d[qp][:, j * 384:j * 384 + wmask],
                              func=AF.Exp, bias=negm4d[qp][:, j:j + 1],
                              accum_out=lsum4d[qp][:, j:j + 1])
                    exp_done[qd] = pl.c["s"].n
                    pl.gset("ssb4_%d" % qp, "s")
                    pl.gset("negm4_%d" % qp, "s")

                def front3(qd):
                    qp = qd % 2
                    pl.wait("v", pl.c["s"], exp_done[qd])
                    pl.gwait("v", "rl4_%d" % qp)
                    pl.op("v", "reciprocal", out=rl4d[qp], in_=lsum4d[qp])
                    pl.gset("lsum4_%d" % qp, "v")
                    pl.gwait("v", "Pb4_%d" % qp)
                    for j in range(4):
                        pl.op("v", "tensor_scalar",
                              out=Pb4d[qp][:, j * 384:j * 384 + wmask],
                              in0=pE4d[qp][:, j * 384:j * 384 + wmask],
                              scalar1=rl4d[qp][:, j:j + 1], scalar2=None,
                              op0=ALU.mult)
                    pb_done[qd] = pl.c["v"].n
                    pl.gset("rl4_%d" % qp, "v")

                sd = {}
                add_done = {}
                exp_done = {}
                sd[0] = scores_quad(0)
                front1(0)
                sd[1] = scores_quad(1)
                front2(0)
                for qd in range(1, 4):
                    front1(qd)
                    if qd < 3:
                        sd[qd + 1] = scores_quad(qd + 1)
                    front2(qd)
                    front3(qd - 1)
                    if qd >= 2:
                        backend_quad(qd - 2)
                front3(3)
                backend_quad(2)
                backend_quad(3)
                sl = tt % 2
                pl.wait("t", pl.c["v"], pl.c["v"].n)
                pl.wait("t", pl.c["s"], pl.c["s"].n)
                pl.wait("t", sst[sl], sst[sl].n)
                pl.gwait("t", "ostg%d" % sl)
                for half2 in range(2):
                    pl.gwait("t", "bank4")
                    pl.gwait("t", "bank5")
                    for hc in range(H):
                        last = hc == H - 1
                        for jj in range(2):
                            js = half2 * 2 + jj
                            pl.op("t", "matmul", bank(4 + jj),
                                  ctxq[hc // 4][:, (hc % 4) * 128:
                                                (hc % 4 + 1) * 128],
                                  woT[hc][:, js * 512:(js + 1) * 512],
                                  start=(hc == 0), stop=last,
                                  inc=(pl.c["t"] if (last and jj == 1) else None))
                    pl.wait("v", pl.c["t"], pl.c["t"].n)
                    for jj in range(2):
                        js = half2 * 2 + jj
                        pl.op("v", "tensor_copy",
                              out=ostg[sl][:, js * 512:(js + 1) * 512],
                              in_=bank(4 + jj))
                    pl.gset("bank4", "v")
                    pl.gset("bank5", "v")
                for q2 in range(4):
                    pl.gset("ctxq%d" % q2, "t")
                pl.gwait("v", "bl")
                pl.op("v", "tensor_tensor", out=junk, in0=ostg[sl],
                      in1=gateR, op=ALU.mult)
                pl.op("v", "reduce_sum", out=bl, in_=junk, axis=AX)
                pl.wait("s", pl.c["v"], pl.c["v"].n)
                pl.op("s", "activation", out=bypAll[:, tt:tt + 1], in_=bl,
                      func=AF.Sigmoid, bias=gbT[:, 0:1])
                pl.gset("bl", "s")
                pl.dma("s", oloc[tt * 128:(tt + 1) * 128, :], ostg[sl][:, :],
                       sst[sl])
                pl.gset("ostg%d" % sl, "v")
            pl.gset("p3qk", "t")

    def p4():
        a = Ar(pbase)
        memT = a.tl(DC, [128, M], BF16)
        qsl0 = a.tl(DC, [128, 512], BF16)
        qsl = [qsl0, qsl0]
        simSd = a.tl(2, [128, M], F32)
        sE = a.t([128, M], BF16)
        sB = a.t([128, M], BF16)
        t8 = a.t([128, 8], F32)
        negm = a.t([128, 1], F32)
        e8 = a.t([128, 8], F32)
        sm = S("p4m")
        sq = [S("p4q0"), S("p4q1")]
        sst = S("p4st")
        for ec in range(DC):
            pl.dma("s", memT[ec][:, :], memN[:, ec * 128:(ec + 1) * 128], sm,
                   transpose=True)
        pl.wait("t", sm)
        for g in range(8):
            sl = g % 2
            pl.gwait("y", "p4xs")
            for ec in range(DC):
                pl.dma("y", qsl[sl][ec][:, :],
                       qmTs[ec * 128:(ec + 1) * 128, g * 512:(g + 1) * 512],
                       sq[sl])
            pl.wait("t", sq[sl])
            for tl4 in range(4):
                tt = g * 4 + tl4
                sp = tt % 2
                simS = simSd[sp]
                for mq in range(8):
                    bk = mq % 4
                    ee = "v" if mq % 2 == 0 else "s"
                    pl.gwait("t", "bank%d" % bk)
                    if mq == 0:
                        pl.gwait("t", "simS%d" % sp)
                    for ec in range(DC):
                        last = ec == DC - 1
                        pl.op("t", "matmul", bank(bk),
                              qsl[sl][ec][:, tl4 * 128:(tl4 + 1) * 128],
                              memT[ec][:, mq * 512:(mq + 1) * 512],
                              start=(ec == 0), stop=last,
                              inc=(pl.c["t"] if last else None))
                    pl.wait(ee, pl.c["t"], pl.c["t"].n)
                    pl.gwait(ee, "simSr%d" % sp)
                    if ee == "v":
                        pl.op("v", "tensor_copy",
                              out=simS[:, mq * 512:(mq + 1) * 512],
                              in_=bank(bk))
                    else:
                        pl.op("s", "activation",
                              out=simS[:, mq * 512:(mq + 1) * 512],
                              in_=bank(bk), func=AF.Copy)
                    pl.gset("bank%d" % bk, ee)
                pl.wait("v", pl.c["s"], pl.c["s"].n)
                pl.gwait("v", "t8")
                pl.op("v", "max", out=t8, in_=simS)
                pl.op("v", "reduce_max", out=negm, in_=t8, axis=AX,
                      negate=True)
                pl.wait("s", pl.c["v"], pl.c["v"].n)
                pl.gwait("s", "sE")
                pl.op("s", "activation", out=e8, in_=t8, func=AF.Exp,
                      bias=negm[:, 0:1], accum_out=l8All[:, tt:tt + 1])
                pl.op("s", "activation", out=sE, in_=simS, func=AF.Exp,
                      bias=negm[:, 0:1])
                pl.wait("v", pl.c["s"], pl.c["s"].n)
                pl.wait("v", sst, sst.n)
                pl.op("v", "scalar_tensor_tensor", out=sB, in0=simS,
                      scalar=t8[:, 7:8], in1=sE, op0=ALU.is_ge, op1=ALU.mult)
                pl.gset("simS%d" % sp, "v")
                pl.gset("simSr%d" % sp, "v")
                pl.gset("sE", "v")
                pl.gset("t8", "v")
                pl.wait("s", pl.c["v"], pl.c["v"].n)
                pl.dma("s", sBs[tt * 128:(tt + 1) * 128, :], sB[:, :], sst)
            pl.gset("p4xs", "t")

    def p5():
        a = Ar(pbase)
        voT = a.tl(MC, [128, D], BF16)
        sBt = a.tl(2, [128, M], BF16)
        sTball = a.t([128, MC * 128], BF16)
        xt = a.t([128, D], F32)
        olt = a.t([128, D], F32)
        o1 = a.t([128, D], F32)
        cent = a.t([128, D], F32)
        y2 = a.t([128, D], BF16)
        yslab = a.t([128, DC * 256], BF16)
        scl = a.t([128, 1], F32)
        small = a.tl(5, [128, 1], F32)
        svo = S("p5vo")
        ssb = [S("p5b0"), S("p5b1")]
        sx = S("p5x")
        sol = S("p5ol")
        so1 = S("p5o1")
        syl = S("p5yl")
        yslab_v = yslab.rearrange("p (dc tl c) -> p dc tl c", dc=DC, tl=2)
        for mc in range(MC):
            pl.dma("y", voT[mc][:, :], vo[mc * 128:(mc + 1) * 128, :], svo)
        pl.wait("t", svo)
        for tt in range(TT):
            sl = tt % 2
            pl.gwait("y", "sBt%d" % sl)
            pl.dma("y", sBt[sl][:, :], sBs[tt * 128:(tt + 1) * 128, :], ssb[sl])
            pl.gwait("y", "xt")
            pl.dma("y", xt[:, :], x_h[tt * 128:(tt + 1) * 128, :], sx)
            pl.gwait("y", "olt")
            pl.dma("y", olt[:, :], oloc[tt * 128:(tt + 1) * 128, :], sol)
            pl.wait("t", ssb[sl])
            pl.gwait("t", "sTb")
            for mc in range(MC):
                k = mc % 8
                hbs = (mc // 8) % 2
                psT = psT_half(hbs)
                if k == 0:
                    pl.gwait("t", "psT%d" % hbs)
                pl.op("t", "transpose", psT[:, k * 128:(k + 1) * 128],
                      sBt[sl][:, mc * 128:(mc + 1) * 128], ident,
                      inc=(pl.c["t"] if k == 7 else None))
                if k == 7:
                    pl.wait("v", pl.c["t"], pl.c["t"].n)
                    pl.op("v", "tensor_copy",
                          out=sTball[:, (mc - 7) * 128:(mc + 1) * 128],
                          in_=psT[:, 0:1024])
                    pl.gset("psT%d" % hbs, "v")
            pl.gset("sBt%d" % sl, "t")
            pl.wait("t", pl.c["v"], pl.c["v"].n)
            for bk in range(4):
                pl.gwait("t", "bank%d" % bk)
            for mc in range(MC):
                for js in range(4):
                    last = mc == MC - 1
                    pl.op("t", "matmul", bank(js),
                          sTball[:, mc * 128:(mc + 1) * 128],
                          voT[mc][:, js * 512:(js + 1) * 512],
                          start=(mc == 0), stop=last,
                          inc=(pl.c["t"] if (last and js == 3) else None))
            pl.gset("sTb", "t")
            pl.op("v", "reciprocal", out=scl, in_=l8All[:, tt:tt + 1])
            pl.op("v", "tensor_tensor", out=scl, in0=scl,
                  in1=bypAll[:, tt:tt + 1], op=ALU.mult)
            pl.wait("v", sx)
            pl.wait("v", sol)
            pl.wait("v", so1, so1.n)
            pl.gwait("v", "o1")
            pl.op("v", "tensor_tensor", out=o1, in0=xt, in1=olt, op=ALU.add,
                  inc=pl.c["v"])
            pl.gset("xt", "v")
            pl.wait("s", pl.c["v"], pl.c["v"].n)
            pl.wait("s", pl.c["t"], pl.c["t"].n)
            for js in range(4):
                pl.op("s", "activation", out=olt[:, js * 512:(js + 1) * 512],
                      in_=bank(js), func=AF.Copy, scale=scl[:, 0:1],
                      inc=(pl.c["s"] if js == 3 else None))
            for bk in range(4):
                pl.gset("bank%d" % bk, "s")
            pl.wait("v", pl.c["s"], pl.c["s"].n)
            pl.op("v", "tensor_tensor", out=o1, in0=o1, in1=olt, op=ALU.add,
                  inc=pl.c["v"])
            pl.gset("olt", "v")
            pl.gset("scl", "v")
            pl.wait("s", pl.c["v"], pl.c["v"].n)
            pl.dma("s", out1[tt * 128:(tt + 1) * 128, :], o1[:, :], so1)
            pl.guards.pop("cent", None)
            pl.guards.pop("lnout", None)
            ln_tile((small[0], small[1], small[2], small[3], small[4], olt),
                    o1, cent, y2)
            pl.gset("o1", "v")
            pl.gset("olt", "s")
            tl = tt % 2
            for hb in range(2):
                psT = psT_half(hb)
                psT_v = psT.rearrange("p (k o c) -> p k o c", k=8, o=1)
                pl.gwait("t", "psT%d" % hb)
                pl.wait("t", pl.c["v"], pl.c["v"].n)
                if tl == 0 and hb == 0:
                    pl.wait("t", syl, syl.n)
                for k in range(8):
                    dc = hb * 8 + k
                    pl.op("t", "transpose", psT[:, k * 128:(k + 1) * 128],
                          y2[:, dc * 128:(dc + 1) * 128], ident,
                          inc=(pl.c["t"] if k == 7 else None))
                pl.wait("v", pl.c["t"], pl.c["t"].n)
                pl.op("v", "tensor_copy",
                      out=yslab_v[:, hb * 8:hb * 8 + 8, tl:tl + 1, :], in_=psT_v)
                pl.gset("psT%d" % hb, "v")
            if tl == 1:
                pl.wait("s", pl.c["v"], pl.c["v"].n)
                g2 = tt // 2
                for dc in range(DC):
                    pl.dma("s", yTs[dc * 128:(dc + 1) * 128,
                                  g2 * 256:(g2 + 1) * 256],
                           yslab[:, dc * 256:(dc + 1) * 256], syl)

    def p6():
        a = Ar(pbase)
        yT = a.tl(DC, [128, T], BF16)
        wgi = a.tl(2, [128, D], BF16)
        wui = a.tl(2, [128, D], BF16)
        wgT = a.t([128, D], BF16)
        wuT = a.t([128, D], BF16)
        gS = a.t([128, 512], BF16)
        uS = a.t([128, 512], BF16)
        hS = a.tl(2, [128, 512], BF16)
        syt = S("p6y")
        sgi = [S("p6g0"), S("p6g1")]
        sui = [S("p6u0"), S("p6u1")]
        sho = [S("p6h0"), S("p6h1")]
        for dc in range(DC):
            pl.dma("y", yT[dc][:, :], yTs[dc * 128:(dc + 1) * 128, :], syt)
        pl.wait("t", syt)
        for ic in range(IC):
            sl = ic % 2
            pl.gwait("y", "p6w%d" % sl)
            pl.dma("y", wgi[sl][:, :], wgN[ic * 128:(ic + 1) * 128, :], sgi[sl])
            pl.dma("y", wui[sl][:, :], wuN[ic * 128:(ic + 1) * 128, :], sui[sl])
            pl.wait("t", sgi[sl])
            pl.wait("t", sui[sl])
            pl.gwait("t", "wT")
            for win, wtb in ((wgi[sl], wgT), (wui[sl], wuT)):
                for hb in range(2):
                    psT = psT_half(hb)
                    pl.gwait("t", "psT%d" % hb)
                    for k in range(8):
                        hc = hb * 8 + k
                        pl.op("t", "transpose", psT[:, k * 128:(k + 1) * 128],
                              win[:, hc * 128:(hc + 1) * 128], ident,
                              inc=(pl.c["t"] if k == 7 else None))
                    pl.wait("v", pl.c["t"], pl.c["t"].n)
                    pl.op("v", "tensor_scalar",
                          out=wtb[:, hb * 1024:(hb + 1) * 1024],
                          in0=psT[:, 0:1024], scalar1=-128.0,
                          scalar2=None, op0=ALU.add)
                    pl.gset("psT%d" % hb, "v")
            pl.gset("p6w%d" % sl, "t")
            pl.wait("t", pl.c["v"], pl.c["v"].n)
            for ts in range(8):
                bg = ts % 3
                bu = 3 + ts % 3
                hsl = ts % 2
                pl.gwait("t", "bank%d" % bg)
                pl.gwait("t", "bank%d" % bu)
                for hc in range(DC):
                    last = hc == DC - 1
                    pl.op("t", "matmul", bank(bg),
                          wgT[:, hc * 128:(hc + 1) * 128],
                          yT[hc][:, ts * 512:(ts + 1) * 512],
                          start=(hc == 0), stop=last)
                    pl.op("t", "matmul", bank(bu),
                          wuT[:, hc * 128:(hc + 1) * 128],
                          yT[hc][:, ts * 512:(ts + 1) * 512],
                          start=(hc == 0), stop=last,
                          inc=(pl.c["t"] if last else None))
                pl.wait("s", pl.c["t"], pl.c["t"].n)
                pl.gwait("s", "gS")
                pl.op("s", "activation", out=gS, in_=bank(bg), func=AF.Silu,
                      scale=gscT[:, ic:ic + 1], inc=pl.c["s"])
                pl.gset("bank%d" % bg, "s")
                pl.wait("v", pl.c["s"], pl.c["s"].n)
                pl.wait("v", sho[hsl], sho[hsl].n)
                pl.gwait("v", "uS")
                pl.op("v", "tensor_scalar", out=uS, in0=bank(bu),
                      scalar1=uscT[:, ic:ic + 1], scalar2=None, op0=ALU.mult)
                pl.op("v", "tensor_tensor", out=hS[hsl], in0=gS, in1=uS,
                      op=ALU.mult, inc=pl.c["v"])
                pl.gset("bank%d" % bu, "v")
                pl.gset("gS", "v")
                pl.gset("uS", "v")
                pl.wait("s", pl.c["v"], pl.c["v"].n)
                pl.dma("s", hTs[ic * 128:(ic + 1) * 128,
                             ts * 512:(ts + 1) * 512], hS[hsl][:, :], sho[hsl])
            pl.gset("wT", "t")

    def pdown(half):
        a = Ar(pbase)
        wdTall = a.t([128, 32 * D], BF16)
        wdT_v = wdTall.rearrange("p (i c) -> p i c", i=32)
        wdT = [wdTall[:, i * D:(i + 1) * D] for i in range(32)]
        wdi0 = a.t([128, 4096], BF16)
        wdi = [wdi0, wdi0]
        hsl = [a.tl(32, [128, 256], BF16) for _ in range(2)]
        o1t = a.t([128, D], F32)
        pAt = a.t([128, D], F32)
        fin = a.tl(2, [128, D], F32)
        swd = [S("pdw0%d" % half), S("pdw1%d" % half)]
        shl = [S("pdh0%d" % half), S("pdh1%d" % half)]
        so1 = S("pdo1%d" % half)
        spa = S("pdpa%d" % half)
        sfin = [S("pdf0%d" % half), S("pdf1%d" % half)]
        ioff = half * 32
        for hc in range(DC):
            sl = hc % 2
            pl.gwait("y", "wdi")
            pl.dma("y", wdi[sl][:, :],
                   wdN[hc * 128:(hc + 1) * 128, half * 4096:(half + 1) * 4096],
                   swd[sl])
            pl.wait("t", swd[sl])
            for qb in range(4):
                hb = qb % 2
                psT = psT_half(hb)
                pl.gwait("t", "psT%d" % hb)
                for k in range(8):
                    icl = qb * 8 + k
                    pl.op("t", "transpose", psT[:, k * 128:(k + 1) * 128],
                          wdi[sl][:, icl * 128:(icl + 1) * 128], ident,
                          inc=(pl.c["t"] if k == 7 else None))
                pl.wait("v", pl.c["t"], pl.c["t"].n)
                pl.op("v", "tensor_copy",
                      out=wdT_v[:, qb * 8:(qb + 1) * 8,
                                hc * 128:(hc + 1) * 128],
                      in_=psT.rearrange("p (k c) -> p k c", k=8))
                pl.gset("psT%d" % hb, "v")
                if qb == 3:
                    pl.gset("wdi", "t")
        pl.wait("t", pl.c["v"], pl.c["v"].n)
        for tp in range(16):
            hp = tp % 2
            pl.gwait("y", "hsl%d" % hp)
            for icl in range(32):
                pl.dma("y", hsl[hp][icl][:, :],
                       hTs[(ioff + icl) * 128:(ioff + icl + 1) * 128,
                           tp * 256:(tp + 1) * 256], shl[hp])
            for tl in range(2):
                tt = tp * 2 + tl
                fsl = tt % 2
                if half == 1:
                    pl.gwait("y", "o1t")
                    pl.dma("y", o1t[:, :], out1[tt * 128:(tt + 1) * 128, :], so1)
                    pl.gwait("y", "pAt")
                    pl.dma("y", pAt[:, :], pAs[tt * 128:(tt + 1) * 128, :], spa)
                if tl == 0:
                    pl.wait("t", shl[hp])
                for bk in range(4):
                    pl.gwait("t", "bank%d" % bk)
                for icl in range(32):
                    for js in range(4):
                        last = icl == 31
                        pl.op("t", "matmul", bank(js),
                              hsl[hp][icl][:, tl * 128:(tl + 1) * 128],
                              wdT[icl][:, js * 512:(js + 1) * 512],
                              start=(icl == 0), stop=last,
                              inc=(pl.c["t"] if (last and js == 3) else None))
                if tl == 1:
                    pl.gset("hsl%d" % hp, "t")
                if half == 0:
                    pl.wait("v", pl.c["t"], pl.c["t"].n)
                    pl.wait("v", sfin[fsl], sfin[fsl].n)
                    for js in range(4):
                        pl.op("v", "tensor_copy",
                              out=fin[fsl][:, js * 512:(js + 1) * 512],
                              in_=bank(js),
                              inc=(pl.c["v"] if js == 3 else None))
                    for bk in range(4):
                        pl.gset("bank%d" % bk, "v")
                    pl.wait("s", pl.c["v"], pl.c["v"].n)
                    pl.dma("s", pAs[tt * 128:(tt + 1) * 128, :], fin[fsl][:, :],
                           sfin[fsl])
                else:
                    pl.wait("v", so1)
                    pl.wait("v", spa)
                    pl.wait("v", sfin[fsl], sfin[fsl].n)
                    pl.op("v", "tensor_tensor", out=fin[fsl], in0=o1t, in1=pAt,
                          op=ALU.add)
                    pl.gset("o1t", "v")
                    pl.gset("pAt", "v")
                    pl.wait("v", pl.c["t"], pl.c["t"].n)
                    pl.op("v", "tensor_tensor", out=fin[fsl], in0=fin[fsl],
                          in1=ps7[:, 0:2048], op=ALU.add, inc=pl.c["v"])
                    for bk in range(4):
                        pl.gset("bank%d" % bk, "v")
                    pl.wait("s", pl.c["v"], pl.c["v"].n)
                    pl.dma("s", y_h[tt * 128:(tt + 1) * 128, :], fin[fsl][:, :],
                           sfin[fsl])

    phases = [
        ("p01", p01),
        ("p0d1", p0d1),
        ("p0d2", p0d2),
        ("p2a", lambda: proj_phase(wqN, True, qTs, 1.0 / float(np.sqrt(HD)),
                                   "pa", conv="g")),
        ("p2b", lambda: proj_phase(wkN, True, kTs, 1.0, "pb", conv="u")),
        ("p2c", lambda: proj_phase(wkN, False, qmTs, 1.0 / float(np.sqrt(D)),
                                   "pc", conv="d")),
        ("p2d", p2d),
        ("p3", p3),
        ("p4", p4),
        ("p5", p5),
        ("p6", p6),
        ("p7", lambda: pdown(0)),
        ("p8", lambda: pdown(1)),
    ]
    upto = _PHASES_UPTO if _PHASES_UPTO is not None else len(phases)
    for name, fn in phases[:upto]:
        if _PHASE_SELECT is not None and name not in _PHASE_SELECT:
            continue
        fn()
        pl.boundary()

    for pi in range(len(pl.marks) - 1):
        with nc.Block() as block:
            emap = {"t": block.tensor, "v": block.vector, "s": block.scalar,
                    "g": block.gpsimd, "y": block.sync}
            for e in ENGS:
                lo = pl.marks[pi][e]
                hi = pl.marks[pi + 1][e]
                if hi == lo:
                    continue
                def body(b, e=e, lo=lo, hi=hi):
                    for f in pl.q[e][lo:hi]:
                        f(b)
                emap[e](body)
    return nc


def kernel(**inputs):
    global _LAST_RESULTS
    nc = _build()
    xs = inputs["L0_attn_hidden_states"]
    base = {
        "wq": inputs["L0_attn_wq"],
        "wk": inputs["L0_attn_wk"],
        "wv": inputs["L0_attn_wv"],
        "wo": inputs["L0_attn_wo"],
        "gw": inputs["L0_attn_gate_w"],
        "gb": inputs["L0_attn_gate_b"],
        "mem": inputs["L0_attn_mem_bank"],
        "fg": inputs["L0_ffn_gate_w"],
        "fgs": inputs["L0_ffn_gate_scale"],
        "fu": inputs["L0_ffn_up_w"],
        "fus": inputs["L0_ffn_up_scale"],
        "fd": inputs["L0_ffn_down_w"],
        "fds": inputs["L0_ffn_down_scale"],
    }
    in_maps = [dict(base, x=np.ascontiguousarray(xs[b])) for b in range(NCORE)]
    res = run_bass_kernel_spmd(nc, in_maps, core_ids=list(range(NCORE)))
    _LAST_RESULTS = res.results
    out = np.stack([np.asarray(r["y"], dtype=np.float32)
                    for r in res.results], axis=0)
    return out


# version 08
# speedup vs baseline: 1.0508x; 1.0356x over previous
import numpy as np
import concourse.bass as bass
import concourse.mybir as mybir
from concourse.bass_utils import run_bass_kernel_spmd

F32 = mybir.dt.float32
BF16 = mybir.dt.bfloat16
I32 = mybir.dt.int32
AF = mybir.ActivationFunctionType
ALU = mybir.AluOpType
AX = mybir.AxisListType.X

NEG = -1.0e30
T, D, H, HD, W, M, I = 4096, 2048, 16, 128, 256, 4096, 8192
TT = T // 128
DC = D // 128
MC = M // 128
IC = I // 128
NCORE = 8

_DBG_OUTS = set()
_PHASES_UPTO = None
_PHASE_SELECT = None
_P0_PARTS = {"masks", "gate", "casts", "scales", "conv_gu", "conv_d"}
_LAST_RESULTS = None

ENGS = ("t", "v", "s", "g", "y")


class _Sem:
    __slots__ = ("h", "n")

    def __init__(self, nc, name):
        self.h = nc.alloc_semaphore(name)
        self.n = 0


class _PL:

    def __init__(self, nc):
        self.nc = nc
        self.q = {e: [] for e in ENGS}
        self.c = {e: _Sem(nc, "c" + e) for e in ENGS}
        self.dirty = {e: set() for e in ENGS}
        self.guards = {}
        self._sems = {}
        self.marks = [{e: 0 for e in ENGS}]

    def sem(self, name):
        if name not in self._sems:
            self._sems[name] = _Sem(self.nc, name)
        return self._sems[name]

    def wait(self, e, sem, val=None):
        v = sem.n if val is None else val
        if v <= 0:
            return
        self.q[e].append(lambda b, s=sem.h, v=v: b.wait_ge(s, v))

    def inc(self, e, sem, n=1):
        sem.n += n
        self.q[e].append(lambda b, s=sem.h, n=n: b.sem_inc(s, n))

    def dma(self, e, out, in_, sem, transpose=False):
        sem.n += 16
        self.dirty[e].add(sem)

        def f(b, o=out, i=in_, s=sem.h, tr=transpose):
            if tr:
                b.dma_start(out=o, in_=i, transpose=True).then_inc(s, 16)
            else:
                b.dma_start(out=o, in_=i).then_inc(s, 16)

        self.q[e].append(f)

    def op(self, e, meth, *a, inc=None, **kw):
        auto = e in ("v", "s", "g")
        if auto:
            self.wait(e, self.c[e], self.c[e].n)
            assert inc is None or inc is self.c[e]
            inc = self.c[e]
        if inc is not None:
            inc.n += 1

        def f(b, meth=meth, a=a, kw=kw, inc=inc):
            r = getattr(b, meth)(*a, **kw)
            if inc is not None:
                r.then_inc(inc.h, 1)

        self.q[e].append(f)

    def gset(self, key, eng):
        self.guards[key] = (eng, self.c[eng].n)

    def gwait(self, e, key):
        if key in self.guards:
            eng, val = self.guards[key]
            if eng == e:
                return
            self.wait(e, self.c[eng], val)

    def boundary(self):
        for e in ENGS:
            for s in self.dirty[e]:
                self.wait(e, s)
            self.dirty[e] = set()
        self.guards = {}
        self.marks.append({e: len(self.q[e]) for e in ENGS})


def _build():
    nc = bass.Bass("TRN2", target_bir_lowering=False)

    def din(name, shape, dt):
        return nc.dram_tensor(name, shape, dt, kind="ExternalInput").ap()

    def dscr(name, shape, dt):
        kind = "ExternalOutput" if name in _DBG_OUTS else "Internal"
        return nc.dram_tensor(name, shape, dt, kind=kind).ap()

    x_h = din("x", [T, D], F32)
    wq_h = din("wq", [D, D], F32)
    wk_h = din("wk", [D, D], F32)
    wv_h = din("wv", [D, D], F32)
    wo_h = din("wo", [D, D], F32)
    gw_h = din("gw", [1, D], F32)
    gb_h = din("gb", [1], F32)
    mem_h = din("mem", [M, D], F32)
    fg_h = din("fg", [I, D], I32)
    fgs_h = din("fgs", [I], F32)
    fu_h = din("fu", [I, D], I32)
    fus_h = din("fus", [I], F32)
    fd_h = din("fd", [D, I], I32)
    fds_h = din("fds", [D], F32)
    y_h = nc.dram_tensor("y", [T, D], F32, kind="ExternalOutput").ap()

    wqN = dscr("wqN", [D, D], BF16)
    wkN = dscr("wkN", [D, D], BF16)
    wvN = dscr("wvN", [D, D], BF16)
    woN = dscr("woN", [D, D], BF16)
    memN = dscr("memN", [M, D], BF16)
    wgN = dscr("wgN", [I, D], BF16)
    wuN = dscr("wuN", [I, D], BF16)
    wdN = dscr("wdN", [D, I], BF16)
    vmT = dscr("vmT", [D, M], BF16)
    vo = dscr("vo", [M, D], BF16)
    xlnT = dscr("xlnT", [D, T], BF16)
    qTs = dscr("qTs", [D, T], BF16)
    kTs = dscr("kTs", [D, T], BF16)
    qmTs = dscr("qmTs", [D, T], BF16)
    vts = dscr("vts", [T, D], BF16)
    oloc = dscr("oloc", [T, D], F32)
    sBs = dscr("sBs", [T, M], BF16)
    out1 = dscr("out1", [T, D], F32)
    gateRs = dscr("gateRs", [128, D], F32)
    yTs = dscr("yTs", [D, T], BF16)
    hTs = dscr("hTs", [I, T], BF16)
    pAs = dscr("pAs", [T, D], F32)

    _uid = [0]

    def sb_at(off, shape, dt):
        _uid[0] += 1
        assert off % 32 == 0
        return nc.alloc_sbuf_tensor_at(
            "sb%d" % _uid[0], shape, dt, offset=off).ap()

    base0 = (nc.sbuf_base + 31) // 32 * 32
    cap = nc.sbuf_top

    class Ar:
        def __init__(self, start):
            self.off = start

        def t(self, shape, dt):
            nbytes = 1
            for s in shape[1:]:
                nbytes *= s
            nbytes *= {F32: 4, BF16: 2, I32: 4}[dt]
            nbytes = (nbytes + 31) // 32 * 32
            r = sb_at(self.off, shape, dt)
            self.off += nbytes
            assert self.off <= cap, f"sbuf overflow {self.off} > {cap}"
            return r

        def tl(self, n, shape, dt):
            return [self.t(shape, dt) for _ in range(n)]

    per = Ar(base0)
    ident = per.t([128, 128], BF16)
    gbT = per.t([128, 1], F32)
    gscT = per.t([128, IC], F32)
    uscT = per.t([128, IC], F32)
    dscT = per.t([128, DC], F32)
    bypAll = per.t([128, TT], F32)
    l8All = per.t([128, TT], F32)
    c_eps = per.t([128, 1], F32)
    c_invD = per.t([128, 1], F32)
    c_sHD = per.t([128, 1], F32)
    c_sD = per.t([128, 1], F32)
    pbase = (per.off + 31) // 32 * 32

    ps7 = nc.alloc_psum_tensor("ps7", [128, 3072], F32).ap()
    psTf = nc.alloc_psum_tensor("psT", [128, 2048], BF16).ap()

    def bank(k, w=512, o=0):
        return ps7[:, 512 * k + o:512 * k + o + w]

    def psT_half(j):
        return psTf[:, 1024 * j:1024 * j + 1024]

    pl = _PL(nc)
    S = pl.sem

    def conv_gu(wsrc, wdst, ar, tag):
        bi = ar.tl(2, [128, D], I32)
        bo = ar.tl(2, [128, D], BF16)
        sbi = [S("cvi0"), S("cvi1")]
        sbo = [S("cvo0"), S("cvo1")]
        state = [0]

        def step():
            ic = state[0]
            if ic >= IC:
                return
            state[0] += 1
            if ic == 0:
                pl.gwait("y", "convbuf")
            sl = ic % 2
            pl.wait("y", sbo[sl])
            pl.dma("y", bi[sl][:, :], wsrc[ic * 128:(ic + 1) * 128, :], sbi[sl])
            pl.wait("v", sbi[sl])
            pl.op("v", "tensor_copy", out=bo[sl], in_=bi[sl])
            pl.wait("s", pl.c["v"], pl.c["v"].n)
            pl.dma("s", wdst[ic * 128:(ic + 1) * 128, :], bo[sl][:, :], sbo[sl])
        return step

    def conv_d(ar):
        di = ar.tl(2, [128, 4096], I32)
        do = ar.tl(2, [128, 4096], BF16)
        sdi = [S("cvi0"), S("cvi1")]
        sdo = [S("cvo0"), S("cvo1")]
        state = [0]

        def step():
            k = state[0]
            if k >= DC * 2:
                return
            state[0] += 1
            if k == 0:
                pl.gwait("y", "convbuf")
            hc, half = k // 2, k % 2
            sl = k % 2
            pl.wait("y", sdo[0])
            pl.wait("y", sdo[1])
            pl.dma("y", di[sl][:, :],
                   fd_h[hc * 128:(hc + 1) * 128,
                        half * 4096:(half + 1) * 4096], sdi[sl])
            pl.wait("v", sdi[sl])
            pl.op("v", "tensor_scalar", out=do[sl], in0=di[sl],
                  scalar1=-128.0, scalar2=dscT[:, hc:hc + 1],
                  op0=ALU.add, op1=ALU.mult)
            pl.wait("s", pl.c["v"], pl.c["v"].n)
            pl.dma("s", wdN[hc * 128:(hc + 1) * 128,
                         half * 4096:(half + 1) * 4096], do[sl][:, :],
                   sdo[sl])
        return step

    def p01():
        a = Ar(pbase)
        pl.op("v", "memset", c_eps, 1e-5)
        pl.op("v", "memset", c_invD, 1.0 / D)
        pl.op("v", "memset", c_sHD, 1.0 / float(np.sqrt(HD)))
        pl.op("v", "memset", c_sD, 1.0 / float(np.sqrt(D)))
        ones1 = a.t([1, 128], F32)
        grow = a.t([1, D], F32)
        gbrow = a.t([1, 1], F32)
        gateR0 = a.t([128, D], F32)
        sq = S("p0q")
        sg = S("p0g")
        pl.op("g", "memset", ident, 1.0, inc=pl.c["g"])
        pl.wait("g", pl.c["g"], pl.c["g"].n)
        pl.op("g", "affine_select", out=ident, in_=ident, pattern=[[1, 128]],
              compare_op=ALU.is_equal, fill=0.0, base=0, channel_multiplier=-1,
              inc=pl.c["g"])
        pl.dma("g", wqN[:, :], wq_h[:, :], sg)
        pl.dma("g", wkN[:, :], wk_h[:, :], sg)
        pl.dma("g", wvN[:, :], wv_h[:, :], sg)
        pl.dma("g", woN[:, :], wo_h[:, :], sg)
        pl.dma("g", memN[:, :], mem_h[:, :], sg)
        pl.dma("y", grow[:, :], gw_h[:, :], sq)
        pl.dma("y", gbrow[:, :], gb_h.rearrange("(a b) -> a b", b=1), sq)
        for c in range(IC):
            pl.dma("y", gscT[:, c:c + 1],
                   fgs_h[c * 128:(c + 1) * 128].rearrange("(p o) -> p o", o=1), sq)
            pl.dma("y", uscT[:, c:c + 1],
                   fus_h[c * 128:(c + 1) * 128].rearrange("(p o) -> p o", o=1), sq)
        for c in range(DC):
            pl.dma("y", dscT[:, c:c + 1],
                   fds_h[c * 128:(c + 1) * 128].rearrange("(p o) -> p o", o=1), sq)
        pl.op("v", "memset", ones1, 1.0, inc=pl.c["v"])
        pl.wait("t", pl.c["v"], pl.c["v"].n)
        pl.wait("t", sq)
        pl.wait("t", pl.c["g"], pl.c["g"].n)
        pl.op("t", "matmul", bank(0, 1), ones1[0:1, :], gbrow[0:1, :],
              start=True, stop=True, inc=pl.c["t"])
        pl.wait("v", pl.c["t"], pl.c["t"].n)
        pl.op("v", "tensor_copy", out=gbT, in_=bank(0, 1))
        pl.gset("bank0", "v")
        for jsb in range(4):
            pl.gwait("t", "bank1")
            pl.op("t", "matmul", bank(1), ones1[0:1, :],
                  grow[0:1, jsb * 512:(jsb + 1) * 512], start=True,
                  stop=True, inc=pl.c["t"])
            pl.wait("v", pl.c["t"], pl.c["t"].n)
            pl.op("v", "tensor_copy",
                  out=gateR0[:, jsb * 512:(jsb + 1) * 512],
                  in_=bank(1))
            pl.gset("bank1", "v")
        pl.wait("s", pl.c["v"], pl.c["v"].n)
        pl.dma("s", gateRs[:, :], gateR0[:, :], S("p0gr"))
        xt = a.tl(2, [128, D], F32)
        sqd = a.t([128, D], F32)
        xln = a.t([128, D], BF16)
        slab = a.t([128, DC * 512], BF16)
        small = a.tl(5, [128, 1], F32)
        sx = [S("lnx0"), S("lnx1")]
        sst = S("lnst")
        slab_v = slab.rearrange("p (dc tl c) -> p dc tl c", dc=DC, tl=4)
        for tt in range(TT):
            sl = tt % 2
            pl.gwait("y", "xt%d" % sl)
            pl.dma("y", xt[sl][:, :], x_h[tt * 128:(tt + 1) * 128, :], sx[sl])
            pl.wait("v", sx[sl])
            pl.guards.pop("cent", None)
            ln_tile((small[0], small[1], small[2], small[3], small[4], sqd),
                    xt[sl], xt[sl], xln)
            pl.gset("xt%d" % sl, "v")
            tl = tt % 4
            for hb in range(2):
                psT = psT_half(hb)
                psT_v = psT.rearrange("p (k o c) -> p k o c", k=8, o=1)
                pl.gwait("t", "psT%d" % hb)
                pl.wait("t", pl.c["v"], pl.c["v"].n)
                if tl == 0 and hb == 0:
                    pl.wait("t", sst, sst.n)
                for k in range(8):
                    pl.op("t", "transpose", psT[:, k * 128:(k + 1) * 128],
                          xln[:, (hb * 8 + k) * 128:(hb * 8 + k + 1) * 128],
                          ident, inc=(pl.c["t"] if k == 7 else None))
                pl.wait("v", pl.c["t"], pl.c["t"].n)
                pl.op("v", "tensor_copy",
                      out=slab_v[:, hb * 8:hb * 8 + 8, tl:tl + 1, :], in_=psT_v)
                pl.gset("psT%d" % hb, "v")
            if tl == 3:
                pl.wait("s", pl.c["v"], pl.c["v"].n)
                g = tt // 4
                for dc in range(DC):
                    pl.dma("s", xlnT[dc * 128:(dc + 1) * 128,
                                   g * 512:(g + 1) * 512],
                           slab[:, dc * 512:(dc + 1) * 512], sst)

    def p0d1():
        a = Ar(pbase)
        wvT = a.tl(DC, [128, D], BF16)
        memTh = a.tl(DC, [128, 2048], BF16)
        stg = a.tl(2, [128, 512], BF16)
        sw = S("d1w")
        sm = S("d1m")
        sst = [S("d1s0"), S("d1s1")]
        for ec in range(DC):
            pl.dma("s", wvT[ec][:, :], wvN[:, ec * 128:(ec + 1) * 128], sw,
                   transpose=True)
        for mh in range(2):
            if mh == 1:
                pl.gwait("s", "memTh")
            for ec in range(DC):
                pl.dma("s", memTh[ec][:, :],
                       memN[mh * 2048:(mh + 1) * 2048, ec * 128:(ec + 1) * 128],
                       sm, transpose=True)
            pl.wait("t", sw)
            pl.wait("t", sm)
            for jc in range(DC):
                for ms in range(4):
                    bk = ms
                    pl.gwait("t", "bank%d" % bk)
                    for ec in range(DC):
                        last = ec == DC - 1
                        pl.op("t", "matmul", bank(bk),
                              wvT[ec][:, jc * 128:(jc + 1) * 128],
                              memTh[ec][:, ms * 512:(ms + 1) * 512],
                              start=(ec == 0), stop=last,
                              inc=(pl.c["t"] if last else None))
                    sl = ms % 2
                    pl.wait("v", pl.c["t"], pl.c["t"].n)
                    pl.wait("v", sst[sl])
                    pl.op("v", "tensor_copy", out=stg[sl], in_=bank(bk),
                          inc=pl.c["v"])
                    pl.gset("bank%d" % bk, "v")
                    pl.wait("s", pl.c["v"], pl.c["v"].n)
                    pl.dma("s", vmT[jc * 128:(jc + 1) * 128,
                                  mh * 2048 + ms * 512:mh * 2048 + (ms + 1) * 512],
                           stg[sl][:, :], sst[sl])
            pl.gset("memTh", "t")

    def p0d2():
        a = Ar(pbase)
        woT = a.tl(DC, [128, D], BF16)
        slab = a.tl(DC, [128, 512], BF16)
        stg = a.tl(2, [128, D], BF16)
        sw = S("d2w")
        sm = S("d2m")
        sst = [S("d2s0"), S("d2s1")]
        for vc in range(DC):
            pl.dma("s", woT[vc][:, :], woN[:, vc * 128:(vc + 1) * 128], sw,
                   transpose=True)
        pl.wait("t", sw)
        for mg in range(8):
            pl.gwait("y", "d2slab")
            for vc in range(DC):
                pl.dma("y", slab[vc][:, :],
                       vmT[vc * 128:(vc + 1) * 128, mg * 512:(mg + 1) * 512], sm)
            pl.wait("t", sm)
            for ml in range(4):
                sl = ml % 2
                pl.gwait("t", "d2stg%d" % sl)
                pl.wait("t", sst[sl], sst[sl].n)
                for bk in range(4):
                    pl.gwait("t", "bank%d" % bk)
                for vc in range(DC):
                    for os in range(4):
                        last = vc == DC - 1
                        pl.op("t", "matmul", bank(os),
                              slab[vc][:, ml * 128:(ml + 1) * 128],
                              woT[vc][:, os * 512:(os + 1) * 512],
                              start=(vc == 0), stop=last,
                              inc=(pl.c["t"] if (last and os == 3) else None))
                pl.wait("v", pl.c["t"], pl.c["t"].n)
                for os in range(4):
                    pl.op("v", "tensor_copy", out=stg[sl][:, os * 512:(os + 1) * 512],
                          in_=bank(os), inc=(pl.c["v"] if os == 3 else None))
                for bk in range(4):
                    pl.gset("bank%d" % bk, "v")
                pl.gset("d2stg%d" % sl, "v")
                pl.wait("s", pl.c["v"], pl.c["v"].n)
                pl.dma("s", vo[(mg * 4 + ml) * 128:(mg * 4 + ml + 1) * 128, :],
                       stg[sl][:, :], sst[sl])
            pl.gset("d2slab", "t")

    def ln_tile(a_bufs, src_sbuf, cent, dst_bf16):
        sums, mean, vs, sdv, rstd, sqd = a_bufs
        pl.op("v", "reduce_sum", out=sums, in_=src_sbuf, axis=AX)
        pl.op("v", "tensor_scalar", out=mean, in0=sums, scalar1=1.0 / D,
              scalar2=None, op0=ALU.mult)
        pl.gwait("v", "cent")
        pl.op("v", "tensor_scalar", out=cent, in0=src_sbuf, scalar1=mean,
              scalar2=None, op0=ALU.subtract)
        pl.wait("s", pl.c["v"], pl.c["v"].n)
        pl.op("s", "activation", out=sqd, in_=cent, func=AF.Square,
              accum_out=vs)
        pl.op("s", "activation", out=sdv, in_=vs, func=AF.Sqrt,
              scale=c_invD[:, 0:1], bias=c_eps[:, 0:1])
        pl.wait("v", pl.c["s"], pl.c["s"].n)
        pl.op("v", "reciprocal", out=rstd, in_=sdv)
        pl.gwait("v", "lnout")
        pl.op("v", "tensor_scalar", out=dst_bf16, in0=cent, scalar1=rstd,
              scalar2=None, op0=ALU.mult)

    def proj_phase(wsrc, transposed, dst, scale, tag, conv=None):
        scale_ap = 1.0
        if scale != 1.0:
            scale_ap = {1.0 / float(np.sqrt(HD)): c_sHD,
                        1.0 / float(np.sqrt(D)): c_sD}[scale][:, 0:1]
        a = Ar(pbase)
        wt = a.tl(DC, [128, D], BF16)
        xs = [a.tl(DC, [128, 512], BF16) for _ in range(2)]
        stg = a.tl(4, [128, 512], BF16)
        cstep = None
        if conv == "g":
            cstep = conv_gu(fg_h, wgN, a, "cvg")
        elif conv == "u":
            cstep = conv_gu(fu_h, wuN, a, "cvu")
        elif conv == "d":
            cstep = conv_d(a)
        pl.gwait("s", "projdone")
        pl.gwait("y", "projdone")
        sw = S("p2w")
        sxs = [S("p2x0"), S("p2x1")]
        sst = [S("p2s%d" % i) for i in range(4)]
        for dc in range(DC):
            if transposed:
                pl.dma("s", wt[dc][:, :], wsrc[:, dc * 128:(dc + 1) * 128], sw,
                       transpose=True)
            else:
                pl.dma("s", wt[dc][:, :], wsrc[dc * 128:(dc + 1) * 128, :], sw)
        pl.wait("t", sw)
        for ts in range(8):
            sl = ts % 2
            pl.gwait("y", tag + "xs%d" % sl)
            for dc in range(DC):
                pl.dma("y", xs[sl][dc][:, :],
                       xlnT[dc * 128:(dc + 1) * 128, ts * 512:(ts + 1) * 512],
                       sxs[sl])
            pl.wait("t", sxs[sl])
            for jc in range(DC):
                bk = jc % 4
                ee = "v" if jc % 2 == 0 else "s"
                pl.gwait("t", "bank%d" % bk)
                pl.wait("t", sst[bk], sst[bk].n)
                for dc in range(DC):
                    last = dc == DC - 1
                    pl.op("t", "matmul", bank(bk),
                          wt[dc][:, jc * 128:(jc + 1) * 128], xs[sl][dc][:, :],
                          start=(dc == 0), stop=last,
                          inc=(pl.c["t"] if last else None))
                pl.wait(ee, pl.c["t"], pl.c["t"].n)
                if ee == "v":
                    pl.op("v", "tensor_scalar", out=stg[bk], in0=bank(bk),
                          scalar1=scale, scalar2=None, op0=ALU.mult,
                          inc=pl.c["v"])
                else:
                    pl.op("s", "activation", out=stg[bk], in_=bank(bk),
                          func=AF.Copy, scale=scale_ap, inc=pl.c["s"])
                pl.gset("bank%d" % bk, ee)
                pl.wait("s", pl.c[ee], pl.c[ee].n)
                pl.dma("s", dst[jc * 128:(jc + 1) * 128, ts * 512:(ts + 1) * 512],
                       stg[bk][:, :], sst[bk])
                if cstep is not None and (jc % 2 == 1):
                    cstep()
            pl.gset(tag + "xs%d" % sl, "t")
        pl.gset("projdone", "t")
        if cstep is not None:
            for _ in range(2 * IC):
                cstep()
            pl.gset("convbuf", "v")

    def p2d():
        a = Ar(pbase)
        pl.gwait("s", "projdone")
        pl.gwait("y", "projdone")
        wvT = a.tl(DC, [128, D], BF16)
        xs = [a.tl(DC, [128, 512], BF16) for _ in range(2)]
        stg = a.tl(2, [128, D], BF16)
        sw = S("p2w")
        sxs = [S("p2x0"), S("p2x1")]
        sst = [S("p2s0"), S("p2s1")]
        for nm in ("p2s2", "p2s3", "cvo0", "cvo1", "cvi0", "cvi1"):
            pl.wait("t", S(nm))
            pl.wait("v", S(nm))
            pl.wait("y", S(nm))
        for dc in range(DC):
            pl.dma("s", wvT[dc][:, :], wvN[:, dc * 128:(dc + 1) * 128], sw,
                   transpose=True)
        pl.wait("t", sw)
        for g in range(8):
            sl = g % 2
            pl.gwait("y", "vtxs%d" % sl)
            for dc in range(DC):
                pl.dma("y", xs[sl][dc][:, :],
                       xlnT[dc * 128:(dc + 1) * 128, g * 512:(g + 1) * 512],
                       sxs[sl])
            pl.wait("t", sxs[sl])
            for tl in range(4):
                tt = g * 4 + tl
                ssl = tt % 2
                pl.wait("t", sst[ssl], sst[ssl].n)
                pl.gwait("t", "vtstg%d" % ssl)
                for bk in range(4):
                    pl.gwait("t", "bank%d" % bk)
                for dc in range(DC):
                    for js in range(4):
                        last = dc == DC - 1
                        pl.op("t", "matmul", bank(js),
                              xs[sl][dc][:, tl * 128:(tl + 1) * 128],
                              wvT[dc][:, js * 512:(js + 1) * 512],
                              start=(dc == 0), stop=last,
                              inc=(pl.c["t"] if (last and js == 3) else None))
                pl.wait("v", pl.c["t"], pl.c["t"].n)
                for js in range(4):
                    pl.op("v", "tensor_copy",
                          out=stg[ssl][:, js * 512:(js + 1) * 512], in_=bank(js),
                          inc=(pl.c["v"] if js == 3 else None))
                for bk in range(4):
                    pl.gset("bank%d" % bk, "v")
                pl.gset("vtstg%d" % ssl, "v")
                pl.wait("s", pl.c["v"], pl.c["v"].n)
                pl.dma("s", vts[tt * 128:(tt + 1) * 128, :], stg[ssl][:, :],
                       sst[ssl])
            pl.gset("vtxs%d" % sl, "t")

    def p3():
        a = Ar(pbase)
        woT = a.tl(DC, [128, D], BF16)
        gateR = a.t([128, D], F32)
        maskG = a.t([128, 384], F32)
        mask1 = a.t([128, 256], F32)
        mask0 = a.t([128, 128], F32)
        m4G = a.t([128, 4 * 384], F32)
        iv = a.t([128, 384], F32)
        tmp = a.t([128, 384], F32)
        qsl = a.tl(H, [128, 512], BF16)
        ksl = a.tl(H, [128, 768], BF16)
        vsl = a.tl(6, [128, D], BF16)
        _ssb_off = a.off
        ssb4d = a.tl(2, [128, 4 * 384], F32)
        pE4d = a.tl(2, [128, 4 * 384], F32)
        Pb4d = a.tl(2, [128, 4 * 384], BF16)
        PT = a.t([128, 4 * 384], BF16)
        ctxq = a.tl(4, [128, 512], BF16)
        ostg = a.tl(2, [128, D], F32)
        negm4d = a.tl(2, [128, 4], F32)
        lsum4d = a.tl(2, [128, 4], F32)
        rl4d = a.tl(2, [128, 4], F32)
        bl = a.t([128, 1], F32)
        junk = sb_at(_ssb_off, [128, D], F32)
        sw = S("p3w")
        sq = S("p3q")
        sk = S("p3k")
        sv = S("p3v")
        sst = [S("p3s0"), S("p3s1")]
        pl.op("g", "iota", iv, [[1, 384]], base=0, channel_multiplier=-1,
              allow_small_or_imprecise_dtypes=True, inc=pl.c["g"])
        pl.wait("v", pl.c["g"], pl.c["g"].n)
        pl.op("v", "tensor_scalar", out=tmp, in0=iv, scalar1=1.0,
              scalar2=None, op0=ALU.is_ge)
        pl.op("v", "tensor_scalar", out=maskG, in0=iv, scalar1=256.0,
              scalar2=None, op0=ALU.is_le)
        pl.op("v", "tensor_tensor", out=maskG, in0=maskG, in1=tmp, op=ALU.mult)
        pl.op("v", "tensor_scalar", out=maskG, in0=maskG, scalar1=-1.0,
              scalar2=-NEG, op0=ALU.add, op1=ALU.mult)
        pl.op("v", "tensor_scalar", out=mask1, in0=iv[:, 0:256], scalar1=128.0,
              scalar2=None, op0=ALU.is_le)
        pl.op("v", "tensor_scalar", out=mask1, in0=mask1, scalar1=-1.0,
              scalar2=-NEG, op0=ALU.add, op1=ALU.mult)
        pl.op("v", "tensor_scalar", out=mask0, in0=iv[:, 0:128], scalar1=0.0,
              scalar2=None, op0=ALU.is_le)
        pl.op("v", "tensor_scalar", out=mask0, in0=mask0, scalar1=-1.0,
              scalar2=-NEG, op0=ALU.add, op1=ALU.mult)
        for j in range(4):
            pl.op("v", "tensor_copy", out=m4G[:, j * 384:(j + 1) * 384],
                  in_=maskG)
        for dc in range(DC):
            pl.dma("s", woT[dc][:, :], woN[:, dc * 128:(dc + 1) * 128], sw,
                   transpose=True)
        pl.dma("y", gateR[:, :], gateRs[:, :], S("p3gr"))
        pl.wait("v", S("p3gr"))
        pl.wait("t", sw)

        for g in range(8):
            for h in range(H):
                pl.gwait("y", "p3qk")
                pl.dma("y", qsl[h][:, :],
                       qTs[h * 128:(h + 1) * 128, g * 512:(g + 1) * 512], sq)
                lo = g * 512 - 256
                off = 0
                if lo < 0:
                    off = -lo
                    lo = 0
                pl.dma("y", ksl[h][:, off:768],
                       kTs[h * 128:(h + 1) * 128, lo:g * 512 + 512], sk)
            for tl4 in range(4):
                tt = g * 4 + tl4
                pl.gwait("y", "p3qk")
                pl.dma("y", vsl[tt % 6][:, :],
                       vts[tt * 128:(tt + 1) * 128, :], sv)
            pl.wait("t", sq)
            pl.wait("t", sk)
            pl.wait("t", sv)
            for tl4 in range(4):
                tt = g * 4 + tl4
                kcc = min(tt + 1, 3)
                fc = tt - kcc + 1
                m4 = m4G
                msk1h = maskG if tt >= 2 else (mask1 if tt == 1 else mask0)
                wmask = kcc * 128
                pb_done = {}

                def scores_quad(qd):
                    for j in range(4):
                        h = qd * 4 + j
                        pl.gwait("t", "sbank%d" % j)
                        for kc in range(kcc):
                            slabc = (fc + kc) - (g * 4 - 2)
                            pl.op("t", "matmul", bank(j, 128, o=kc * 128),
                                  qsl[h][:, tl4 * 128:(tl4 + 1) * 128],
                                  ksl[h][:, slabc * 128:(slabc + 1) * 128],
                                  start=True, stop=True,
                                  inc=(pl.c["t"] if (kc == kcc - 1 and j == 3)
                                       else None))
                    return pl.c["t"].n

                def backend_quad(qd):
                    qp = qd % 2
                    Pb4 = Pb4d[qp]
                    pl.wait("t", pl.c["v"], pb_done[qd])
                    for hb2 in range(2):
                        psT = psT_half(hb2)
                        pl.gwait("t", "psT%d" % hb2)
                        for jj in range(2):
                            j = hb2 * 2 + jj
                            for kc in range(kcc):
                                lastt = (jj == 1 and kc == kcc - 1)
                                pl.op("t", "transpose",
                                      psT[:, (jj * kcc + kc) * 128:
                                          (jj * kcc + kc + 1) * 128],
                                      Pb4[:, j * 384 + kc * 128:
                                          j * 384 + (kc + 1) * 128],
                                      ident,
                                      inc=(pl.c["t"] if lastt else None))
                        pl.wait("v", pl.c["t"], pl.c["t"].n)
                        pl.gwait("v", "PTh%d" % hb2)
                        pl.op("v", "tensor_copy",
                              out=PT[:, hb2 * 768:hb2 * 768 + 2 * wmask],
                              in_=psT[:, 0:2 * wmask])
                        pl.gset("psT%d" % hb2, "v")
                    pl.gset("Pb4_%d" % qp, "t")
                    pl.wait("t", pl.c["v"], pl.c["v"].n)
                    pl.gwait("t", "bank4")
                    for j in range(4):
                        h = qd * 4 + j
                        hb2, jj = j // 2, j % 2
                        for kc in range(kcc):
                            pl.op("t", "matmul", bank(4, 128, o=j * 128),
                                  vsl[(fc + kc) % 6][:, h * 128:(h + 1) * 128],
                                  PT[:, hb2 * 768 + (jj * kcc + kc) * 128:
                                      hb2 * 768 + (jj * kcc + kc + 1) * 128],
                                  start=(kc == 0), stop=(kc == kcc - 1),
                                  inc=(pl.c["t"] if (j == 3 and kc == kcc - 1)
                                       else None))
                    pl.gset("PTh0", "t")
                    pl.gset("PTh1", "t")
                    pl.wait("v", pl.c["t"], pl.c["t"].n)
                    pl.gwait("v", "ctxq%d" % qd)
                    pl.op("v", "tensor_copy", out=ctxq[qd], in_=bank(4))
                    pl.gset("bank4", "v")

                def front1(qd):
                    qp = qd % 2
                    ps_v = ps7[:, 0:2048].rearrange(
                        "p (h c) -> p h c", h=4)[:, :, 0:wmask]
                    ssb_v = ssb4d[qp].rearrange(
                        "p (h c) -> p h c", h=4)[:, :, 0:wmask]
                    m4_v = m4.rearrange(
                        "p (h c) -> p h c", h=4)[:, :, 0:wmask]
                    pl.wait("v", pl.c["t"], sd[qd])
                    pl.gwait("v", "ssb4_%d" % qp)
                    pl.gwait("v", "negm4_%d" % qp)
                    if tt >= 2:
                        pl.op("v", "tensor_tensor", out=ssb_v, in0=ps_v,
                              in1=m4_v, op=ALU.add)
                    else:
                        for j in range(4):
                            pl.op("v", "tensor_tensor",
                                  out=ssb4d[qp][:, j * 384:j * 384 + wmask],
                                  in0=bank(j, wmask),
                                  in1=msk1h[:, 0:wmask], op=ALU.add)
                    pl.op("v", "reduce_max", out=negm4d[qp], in_=ssb_v,
                          axis=AX, negate=True)
                    add_done[qd] = pl.c["v"].n
                    for j in range(4):
                        pl.gset("sbank%d" % j, "v")

                def front2(qd):
                    qp = qd % 2
                    pl.wait("s", pl.c["v"], add_done[qd])
                    pl.gwait("s", "lsum4_%d" % qp)
                    for j in range(4):
                        pl.op("s", "activation",
                              out=pE4d[qp][:, j * 384:j * 384 + wmask],
                              in_=ssb4d[qp][:, j * 384:j * 384 + wmask],
                              func=AF.Exp, bias=negm4d[qp][:, j:j + 1],
                              accum_out=lsum4d[qp][:, j:j + 1])
                    exp_done[qd] = pl.c["s"].n
                    pl.gset("ssb4_%d" % qp, "s")
                    pl.gset("negm4_%d" % qp, "s")

                def front3(qd):
                    qp = qd % 2
                    pl.wait("v", pl.c["s"], exp_done[qd])
                    pl.gwait("v", "rl4_%d" % qp)
                    pl.op("v", "reciprocal", out=rl4d[qp], in_=lsum4d[qp])
                    pl.gset("lsum4_%d" % qp, "v")
                    pl.gwait("v", "Pb4_%d" % qp)
                    for j in range(4):
                        pl.op("v", "tensor_scalar",
                              out=Pb4d[qp][:, j * 384:j * 384 + wmask],
                              in0=pE4d[qp][:, j * 384:j * 384 + wmask],
                              scalar1=rl4d[qp][:, j:j + 1], scalar2=None,
                              op0=ALU.mult)
                    pb_done[qd] = pl.c["v"].n
                    pl.gset("rl4_%d" % qp, "v")

                sd = {}
                add_done = {}
                exp_done = {}
                sd[0] = scores_quad(0)
                front1(0)
                sd[1] = scores_quad(1)
                front2(0)
                for qd in range(1, 4):
                    front1(qd)
                    if qd < 3:
                        sd[qd + 1] = scores_quad(qd + 1)
                    front2(qd)
                    front3(qd - 1)
                    if qd >= 2:
                        backend_quad(qd - 2)
                front3(3)
                backend_quad(2)
                backend_quad(3)
                sl = tt % 2
                pl.wait("t", pl.c["v"], pl.c["v"].n)
                pl.wait("t", pl.c["s"], pl.c["s"].n)
                pl.wait("t", sst[sl], sst[sl].n)
                pl.gwait("t", "ostg%d" % sl)
                for half2 in range(2):
                    pl.gwait("t", "bank4")
                    pl.gwait("t", "bank5")
                    for hc in range(H):
                        last = hc == H - 1
                        for jj in range(2):
                            js = half2 * 2 + jj
                            pl.op("t", "matmul", bank(4 + jj),
                                  ctxq[hc // 4][:, (hc % 4) * 128:
                                                (hc % 4 + 1) * 128],
                                  woT[hc][:, js * 512:(js + 1) * 512],
                                  start=(hc == 0), stop=last,
                                  inc=(pl.c["t"] if (last and jj == 1) else None))
                    pl.wait("v", pl.c["t"], pl.c["t"].n)
                    for jj in range(2):
                        js = half2 * 2 + jj
                        pl.op("v", "tensor_copy",
                              out=ostg[sl][:, js * 512:(js + 1) * 512],
                              in_=bank(4 + jj))
                    pl.gset("bank4", "v")
                    pl.gset("bank5", "v")
                for q2 in range(4):
                    pl.gset("ctxq%d" % q2, "t")
                pl.gwait("v", "bl")
                pl.op("v", "tensor_tensor", out=junk, in0=ostg[sl],
                      in1=gateR, op=ALU.mult)
                pl.op("v", "reduce_sum", out=bl, in_=junk, axis=AX)
                pl.wait("s", pl.c["v"], pl.c["v"].n)
                pl.op("s", "activation", out=bypAll[:, tt:tt + 1], in_=bl,
                      func=AF.Sigmoid, bias=gbT[:, 0:1])
                pl.gset("bl", "s")
                pl.dma("s", oloc[tt * 128:(tt + 1) * 128, :], ostg[sl][:, :],
                       sst[sl])
                pl.gset("ostg%d" % sl, "v")
            pl.gset("p3qk", "t")

    def p4():
        a = Ar(pbase)
        memT = a.tl(DC, [128, M], BF16)
        qsl0 = a.tl(DC, [128, 512], BF16)
        qsl = [qsl0, qsl0]
        simSd = a.tl(2, [128, M], F32)
        sE = a.t([128, M], BF16)
        sB = a.t([128, M], BF16)
        t8 = a.t([128, 8], F32)
        negm = a.t([128, 1], F32)
        e8 = a.t([128, 8], F32)
        sm = S("p4m")
        sq = [S("p4q0"), S("p4q1")]
        sst = S("p4st")
        for ec in range(DC):
            pl.dma("s", memT[ec][:, :], memN[:, ec * 128:(ec + 1) * 128], sm,
                   transpose=True)
        pl.wait("t", sm)
        for g in range(8):
            sl = g % 2
            pl.gwait("y", "p4xs")
            for ec in range(DC):
                pl.dma("y", qsl[sl][ec][:, :],
                       qmTs[ec * 128:(ec + 1) * 128, g * 512:(g + 1) * 512],
                       sq[sl])
            pl.wait("t", sq[sl])
            for tl4 in range(4):
                tt = g * 4 + tl4
                sp = tt % 2
                simS = simSd[sp]
                for mq in range(8):
                    bk = mq % 4
                    ee = "v" if mq % 2 == 0 else "s"
                    pl.gwait("t", "bank%d" % bk)
                    if mq == 0:
                        pl.gwait("t", "simS%d" % sp)
                    for ec in range(DC):
                        last = ec == DC - 1
                        pl.op("t", "matmul", bank(bk),
                              qsl[sl][ec][:, tl4 * 128:(tl4 + 1) * 128],
                              memT[ec][:, mq * 512:(mq + 1) * 512],
                              start=(ec == 0), stop=last,
                              inc=(pl.c["t"] if last else None))
                    pl.wait(ee, pl.c["t"], pl.c["t"].n)
                    pl.gwait(ee, "simSr%d" % sp)
                    if ee == "v":
                        pl.op("v", "tensor_copy",
                              out=simS[:, mq * 512:(mq + 1) * 512],
                              in_=bank(bk))
                    else:
                        pl.op("s", "activation",
                              out=simS[:, mq * 512:(mq + 1) * 512],
                              in_=bank(bk), func=AF.Copy)
                    pl.gset("bank%d" % bk, ee)
                pl.wait("v", pl.c["s"], pl.c["s"].n)
                pl.gwait("v", "t8")
                pl.op("v", "max", out=t8, in_=simS)
                pl.op("v", "reduce_max", out=negm, in_=t8, axis=AX,
                      negate=True)
                pl.wait("s", pl.c["v"], pl.c["v"].n)
                pl.gwait("s", "sE")
                pl.op("s", "activation", out=e8, in_=t8, func=AF.Exp,
                      bias=negm[:, 0:1], accum_out=l8All[:, tt:tt + 1])
                pl.op("s", "activation", out=sE, in_=simS, func=AF.Exp,
                      bias=negm[:, 0:1])
                pl.wait("v", pl.c["s"], pl.c["s"].n)
                pl.wait("v", sst, sst.n)
                pl.op("v", "scalar_tensor_tensor", out=sB, in0=simS,
                      scalar=t8[:, 7:8], in1=sE, op0=ALU.is_ge, op1=ALU.mult)
                pl.gset("simS%d" % sp, "v")
                pl.gset("simSr%d" % sp, "v")
                pl.gset("sE", "v")
                pl.gset("t8", "v")
                pl.wait("s", pl.c["v"], pl.c["v"].n)
                pl.dma("s", sBs[tt * 128:(tt + 1) * 128, :], sB[:, :], sst)
            pl.gset("p4xs", "t")

    def p5():
        a = Ar(pbase)
        voT = a.tl(MC, [128, D], BF16)
        sBt = a.tl(2, [128, M], BF16)
        sTball = a.t([128, MC * 128], BF16)
        xt = a.t([128, D], F32)
        olt = a.t([128, D], F32)
        o1 = a.t([128, D], F32)
        cent = a.t([128, D], F32)
        y2d = a.tl(2, [128, D], BF16)
        yslab = a.t([128, DC * 256], BF16)
        scl = a.t([128, 1], F32)
        small = a.tl(5, [128, 1], F32)
        svo = S("p5vo")
        ssb = [S("p5b0"), S("p5b1")]
        sx = S("p5x")
        sol = S("p5ol")
        so1 = S("p5o1")
        syl = S("p5yl")
        yslab_v = yslab.rearrange("p (dc tl c) -> p dc tl c", dc=DC, tl=2)
        for mc in range(MC):
            pl.dma("y", voT[mc][:, :], vo[mc * 128:(mc + 1) * 128, :], svo)
        pl.wait("t", svo)
        y2_done = {}

        def ytrans(tw):
            if tw < 0:
                return
            tl = tw % 2
            for hb in range(2):
                psT = psT_half(hb)
                psT_v = psT.rearrange("p (k o c) -> p k o c", k=8, o=1)
                pl.gwait("t", "psT%d" % hb)
                pl.wait("t", pl.c["v"], y2_done[tw])
                if tl == 0 and hb == 0:
                    pl.wait("t", syl, syl.n)
                for k in range(8):
                    dc = hb * 8 + k
                    pl.op("t", "transpose", psT[:, k * 128:(k + 1) * 128],
                          y2d[tl][:, dc * 128:(dc + 1) * 128], ident,
                          inc=(pl.c["t"] if k == 7 else None))
                pl.wait("v", pl.c["t"], pl.c["t"].n)
                pl.op("v", "tensor_copy",
                      out=yslab_v[:, hb * 8:hb * 8 + 8, tl:tl + 1, :],
                      in_=psT_v)
                pl.gset("psT%d" % hb, "v")
            pl.gset("y2_%d" % tl, "t")
            if tl == 1:
                pl.wait("s", pl.c["v"], pl.c["v"].n)
                g2 = tw // 2
                for dc in range(DC):
                    pl.dma("s", yTs[dc * 128:(dc + 1) * 128,
                                  g2 * 256:(g2 + 1) * 256],
                           yslab[:, dc * 256:(dc + 1) * 256], syl)

        for tt in range(TT):
            sl = tt % 2
            pl.gwait("y", "sBt%d" % sl)
            pl.dma("y", sBt[sl][:, :], sBs[tt * 128:(tt + 1) * 128, :], ssb[sl])
            pl.gwait("y", "xt")
            pl.dma("y", xt[:, :], x_h[tt * 128:(tt + 1) * 128, :], sx)
            pl.gwait("y", "olt")
            pl.dma("y", olt[:, :], oloc[tt * 128:(tt + 1) * 128, :], sol)
            pl.wait("t", ssb[sl])
            pl.gwait("t", "sTb")
            for mc in range(MC):
                k = mc % 8
                hbs = (mc // 8) % 2
                psT = psT_half(hbs)
                if k == 0:
                    pl.gwait("t", "psT%d" % hbs)
                pl.op("t", "transpose", psT[:, k * 128:(k + 1) * 128],
                      sBt[sl][:, mc * 128:(mc + 1) * 128], ident,
                      inc=(pl.c["t"] if k == 7 else None))
                if k == 7:
                    pl.wait("v", pl.c["t"], pl.c["t"].n)
                    pl.op("v", "tensor_copy",
                          out=sTball[:, (mc - 7) * 128:(mc + 1) * 128],
                          in_=psT[:, 0:1024])
                    pl.gset("psT%d" % hbs, "v")
            pl.gset("sBt%d" % sl, "t")
            pl.wait("t", pl.c["v"], pl.c["v"].n)
            for bk in range(4):
                pl.gwait("t", "bank%d" % bk)
            for mc in range(MC):
                for js in range(4):
                    last = mc == MC - 1
                    pl.op("t", "matmul", bank(js),
                          sTball[:, mc * 128:(mc + 1) * 128],
                          voT[mc][:, js * 512:(js + 1) * 512],
                          start=(mc == 0), stop=last,
                          inc=(pl.c["t"] if (last and js == 3) else None))
            pl.gset("sTb", "t")
            ytrans(tt - 1)
            pl.op("v", "reciprocal", out=scl, in_=l8All[:, tt:tt + 1])
            pl.op("v", "tensor_tensor", out=scl, in0=scl,
                  in1=bypAll[:, tt:tt + 1], op=ALU.mult)
            pl.wait("v", sx)
            pl.wait("v", sol)
            pl.wait("v", so1, so1.n)
            pl.gwait("v", "o1")
            pl.op("v", "tensor_tensor", out=o1, in0=xt, in1=olt, op=ALU.add,
                  inc=pl.c["v"])
            pl.gset("xt", "v")
            pl.wait("s", pl.c["v"], pl.c["v"].n)
            pl.wait("s", pl.c["t"], pl.c["t"].n)
            for js in range(4):
                pl.op("s", "activation", out=olt[:, js * 512:(js + 1) * 512],
                      in_=bank(js), func=AF.Copy, scale=scl[:, 0:1],
                      inc=(pl.c["s"] if js == 3 else None))
            for bk in range(4):
                pl.gset("bank%d" % bk, "s")
            pl.wait("v", pl.c["s"], pl.c["s"].n)
            pl.op("v", "tensor_tensor", out=o1, in0=o1, in1=olt, op=ALU.add,
                  inc=pl.c["v"])
            pl.gset("olt", "v")
            pl.gset("scl", "v")
            pl.wait("s", pl.c["v"], pl.c["v"].n)
            pl.dma("s", out1[tt * 128:(tt + 1) * 128, :], o1[:, :], so1)
            pl.guards.pop("cent", None)
            pl.guards.pop("lnout", None)
            pl.gwait("v", "y2_%d" % (tt % 2))
            ln_tile((small[0], small[1], small[2], small[3], small[4], olt),
                    o1, cent, y2d[tt % 2])
            y2_done[tt] = pl.c["v"].n
            pl.gset("o1", "v")
            pl.gset("olt", "s")

        ytrans(TT - 1)

    def p6():
        a = Ar(pbase)
        yT = a.tl(DC, [128, T], BF16)
        wgi = a.tl(2, [128, D], BF16)
        wui = a.tl(2, [128, D], BF16)
        wgT = a.t([128, D], BF16)
        wuT = a.t([128, D], BF16)
        gS = a.t([128, 512], BF16)
        uS = a.t([128, 512], BF16)
        hS = a.tl(2, [128, 512], BF16)
        syt = S("p6y")
        sgi = [S("p6g0"), S("p6g1")]
        sui = [S("p6u0"), S("p6u1")]
        sho = [S("p6h0"), S("p6h1")]
        for dc in range(DC):
            pl.dma("y", yT[dc][:, :], yTs[dc * 128:(dc + 1) * 128, :], syt)
        pl.wait("t", syt)
        for ic in range(IC):
            sl = ic % 2
            pl.gwait("y", "p6w%d" % sl)
            pl.dma("y", wgi[sl][:, :], wgN[ic * 128:(ic + 1) * 128, :], sgi[sl])
            pl.dma("y", wui[sl][:, :], wuN[ic * 128:(ic + 1) * 128, :], sui[sl])
            pl.wait("t", sgi[sl])
            pl.wait("t", sui[sl])
            pl.gwait("t", "wT")
            for win, wtb in ((wgi[sl], wgT), (wui[sl], wuT)):
                for hb in range(2):
                    psT = psT_half(hb)
                    pl.gwait("t", "psT%d" % hb)
                    for k in range(8):
                        hc = hb * 8 + k
                        pl.op("t", "transpose", psT[:, k * 128:(k + 1) * 128],
                              win[:, hc * 128:(hc + 1) * 128], ident,
                              inc=(pl.c["t"] if k == 7 else None))
                    pl.wait("v", pl.c["t"], pl.c["t"].n)
                    pl.op("v", "tensor_scalar",
                          out=wtb[:, hb * 1024:(hb + 1) * 1024],
                          in0=psT[:, 0:1024], scalar1=-128.0,
                          scalar2=None, op0=ALU.add)
                    pl.gset("psT%d" % hb, "v")
            pl.gset("p6w%d" % sl, "t")
            pl.wait("t", pl.c["v"], pl.c["v"].n)
            for ts in range(8):
                bg = ts % 3
                bu = 3 + ts % 3
                hsl = ts % 2
                pl.gwait("t", "bank%d" % bg)
                pl.gwait("t", "bank%d" % bu)
                for hc in range(DC):
                    last = hc == DC - 1
                    pl.op("t", "matmul", bank(bg),
                          wgT[:, hc * 128:(hc + 1) * 128],
                          yT[hc][:, ts * 512:(ts + 1) * 512],
                          start=(hc == 0), stop=last)
                    pl.op("t", "matmul", bank(bu),
                          wuT[:, hc * 128:(hc + 1) * 128],
                          yT[hc][:, ts * 512:(ts + 1) * 512],
                          start=(hc == 0), stop=last,
                          inc=(pl.c["t"] if last else None))
                pl.wait("s", pl.c["t"], pl.c["t"].n)
                pl.gwait("s", "gS")
                pl.op("s", "activation", out=gS, in_=bank(bg), func=AF.Silu,
                      scale=gscT[:, ic:ic + 1], inc=pl.c["s"])
                pl.gset("bank%d" % bg, "s")
                pl.wait("v", pl.c["s"], pl.c["s"].n)
                pl.wait("v", sho[hsl], sho[hsl].n)
                pl.gwait("v", "uS")
                pl.op("v", "tensor_scalar", out=uS, in0=bank(bu),
                      scalar1=uscT[:, ic:ic + 1], scalar2=None, op0=ALU.mult)
                pl.op("v", "tensor_tensor", out=hS[hsl], in0=gS, in1=uS,
                      op=ALU.mult, inc=pl.c["v"])
                pl.gset("bank%d" % bu, "v")
                pl.gset("gS", "v")
                pl.gset("uS", "v")
                pl.wait("s", pl.c["v"], pl.c["v"].n)
                pl.dma("s", hTs[ic * 128:(ic + 1) * 128,
                             ts * 512:(ts + 1) * 512], hS[hsl][:, :], sho[hsl])
            pl.gset("wT", "t")

    def pdown(half):
        a = Ar(pbase)
        wdTall = a.t([128, 32 * D], BF16)
        wdT_v = wdTall.rearrange("p (i c) -> p i c", i=32)
        wdT = [wdTall[:, i * D:(i + 1) * D] for i in range(32)]
        wdi0 = a.t([128, 4096], BF16)
        wdi = [wdi0, wdi0]
        hsl = [a.tl(32, [128, 256], BF16) for _ in range(2)]
        o1t = a.t([128, D], F32)
        pAt = a.t([128, D], F32)
        fin = a.tl(2, [128, D], F32)
        swd = [S("pdw0%d" % half), S("pdw1%d" % half)]
        shl = [S("pdh0%d" % half), S("pdh1%d" % half)]
        so1 = S("pdo1%d" % half)
        spa = S("pdpa%d" % half)
        sfin = [S("pdf0%d" % half), S("pdf1%d" % half)]
        ioff = half * 32
        for hc in range(DC):
            sl = hc % 2
            pl.gwait("y", "wdi")
            pl.dma("y", wdi[sl][:, :],
                   wdN[hc * 128:(hc + 1) * 128, half * 4096:(half + 1) * 4096],
                   swd[sl])
            pl.wait("t", swd[sl])
            for qb in range(4):
                hb = qb % 2
                psT = psT_half(hb)
                pl.gwait("t", "psT%d" % hb)
                for k in range(8):
                    icl = qb * 8 + k
                    pl.op("t", "transpose", psT[:, k * 128:(k + 1) * 128],
                          wdi[sl][:, icl * 128:(icl + 1) * 128], ident,
                          inc=(pl.c["t"] if k == 7 else None))
                pl.wait("v", pl.c["t"], pl.c["t"].n)
                pl.op("v", "tensor_copy",
                      out=wdT_v[:, qb * 8:(qb + 1) * 8,
                                hc * 128:(hc + 1) * 128],
                      in_=psT.rearrange("p (k c) -> p k c", k=8))
                pl.gset("psT%d" % hb, "v")
                if qb == 3:
                    pl.gset("wdi", "t")
        pl.wait("t", pl.c["v"], pl.c["v"].n)
        for tp in range(16):
            hp = tp % 2
            pl.gwait("y", "hsl%d" % hp)
            for icl in range(32):
                pl.dma("y", hsl[hp][icl][:, :],
                       hTs[(ioff + icl) * 128:(ioff + icl + 1) * 128,
                           tp * 256:(tp + 1) * 256], shl[hp])
            for tl in range(2):
                tt = tp * 2 + tl
                fsl = tt % 2
                if half == 1:
                    pl.gwait("y", "o1t")
                    pl.dma("y", o1t[:, :], out1[tt * 128:(tt + 1) * 128, :], so1)
                    pl.gwait("y", "pAt")
                    pl.dma("y", pAt[:, :], pAs[tt * 128:(tt + 1) * 128, :], spa)
                if tl == 0:
                    pl.wait("t", shl[hp])
                for bk in range(4):
                    pl.gwait("t", "bank%d" % bk)
                for icl in range(32):
                    for js in range(4):
                        last = icl == 31
                        pl.op("t", "matmul", bank(js),
                              hsl[hp][icl][:, tl * 128:(tl + 1) * 128],
                              wdT[icl][:, js * 512:(js + 1) * 512],
                              start=(icl == 0), stop=last,
                              inc=(pl.c["t"] if (last and js == 3) else None))
                if tl == 1:
                    pl.gset("hsl%d" % hp, "t")
                if half == 0:
                    pl.wait("v", pl.c["t"], pl.c["t"].n)
                    pl.wait("v", sfin[fsl], sfin[fsl].n)
                    for js in range(4):
                        pl.op("v", "tensor_copy",
                              out=fin[fsl][:, js * 512:(js + 1) * 512],
                              in_=bank(js),
                              inc=(pl.c["v"] if js == 3 else None))
                    for bk in range(4):
                        pl.gset("bank%d" % bk, "v")
                    pl.wait("s", pl.c["v"], pl.c["v"].n)
                    pl.dma("s", pAs[tt * 128:(tt + 1) * 128, :], fin[fsl][:, :],
                           sfin[fsl])
                else:
                    pl.wait("v", so1)
                    pl.wait("v", spa)
                    pl.wait("v", sfin[fsl], sfin[fsl].n)
                    pl.op("v", "tensor_tensor", out=fin[fsl], in0=o1t, in1=pAt,
                          op=ALU.add)
                    pl.gset("o1t", "v")
                    pl.gset("pAt", "v")
                    pl.wait("v", pl.c["t"], pl.c["t"].n)
                    pl.op("v", "tensor_tensor", out=fin[fsl], in0=fin[fsl],
                          in1=ps7[:, 0:2048], op=ALU.add, inc=pl.c["v"])
                    for bk in range(4):
                        pl.gset("bank%d" % bk, "v")
                    pl.wait("s", pl.c["v"], pl.c["v"].n)
                    pl.dma("s", y_h[tt * 128:(tt + 1) * 128, :], fin[fsl][:, :],
                           sfin[fsl])

    phases = [
        ("p01", p01),
        ("p0d1", p0d1),
        ("p0d2", p0d2),
        ("p2a", lambda: proj_phase(wqN, True, qTs, 1.0 / float(np.sqrt(HD)),
                                   "pa", conv="g")),
        ("p2b", lambda: proj_phase(wkN, True, kTs, 1.0, "pb", conv="u")),
        ("p2c", lambda: proj_phase(wkN, False, qmTs, 1.0 / float(np.sqrt(D)),
                                   "pc", conv="d")),
        ("p2d", p2d),
        ("p3", p3),
        ("p4", p4),
        ("p5", p5),
        ("p6", p6),
        ("p7", lambda: pdown(0)),
        ("p8", lambda: pdown(1)),
    ]
    upto = _PHASES_UPTO if _PHASES_UPTO is not None else len(phases)
    for name, fn in phases[:upto]:
        if _PHASE_SELECT is not None and name not in _PHASE_SELECT:
            continue
        fn()
        pl.boundary()

    for pi in range(len(pl.marks) - 1):
        with nc.Block() as block:
            emap = {"t": block.tensor, "v": block.vector, "s": block.scalar,
                    "g": block.gpsimd, "y": block.sync}
            for e in ENGS:
                lo = pl.marks[pi][e]
                hi = pl.marks[pi + 1][e]
                if hi == lo:
                    continue
                def body(b, e=e, lo=lo, hi=hi):
                    for f in pl.q[e][lo:hi]:
                        f(b)
                emap[e](body)
    return nc


def kernel(**inputs):
    global _LAST_RESULTS
    nc = _build()
    xs = inputs["L0_attn_hidden_states"]
    base = {
        "wq": inputs["L0_attn_wq"],
        "wk": inputs["L0_attn_wk"],
        "wv": inputs["L0_attn_wv"],
        "wo": inputs["L0_attn_wo"],
        "gw": inputs["L0_attn_gate_w"],
        "gb": inputs["L0_attn_gate_b"],
        "mem": inputs["L0_attn_mem_bank"],
        "fg": inputs["L0_ffn_gate_w"],
        "fgs": inputs["L0_ffn_gate_scale"],
        "fu": inputs["L0_ffn_up_w"],
        "fus": inputs["L0_ffn_up_scale"],
        "fd": inputs["L0_ffn_down_w"],
        "fds": inputs["L0_ffn_down_scale"],
    }
    in_maps = [dict(base, x=np.ascontiguousarray(xs[b])) for b in range(NCORE)]
    res = run_bass_kernel_spmd(nc, in_maps, core_ids=list(range(NCORE)))
    _LAST_RESULTS = res.results
    out = np.stack([np.asarray(r["y"], dtype=np.float32)
                    for r in res.results], axis=0)
    return out
